# version 01 (final)
# baseline (speedup 1.0000x reference)
import numpy as np
import ml_dtypes

import concourse.bass as bass
import concourse.mybir as mybir
from concourse.bass_utils import run_bass_kernel_spmd

F32 = mybir.dt.float32
BF16 = mybir.dt.bfloat16
F16 = mybir.dt.float16
I32 = mybir.dt.int32
U8 = mybir.dt.uint8
AF = mybir.ActivationFunctionType
ALU = mybir.AluOpType
AX = mybir.AxisListType.X
BF = ml_dtypes.bfloat16

NCORES = 8
D = 2048
NT = 2048
PN = 512
NP = NT // PN
KT = 16
EPS_LN = 1e-5
EPS_GN = 1e-5
EPS_RMS = 1e-6
H1 = 32
D1 = 64
BS = 256
HID = 5632
HT = 44

_rate = (1.0 / 2.0 ** (8.0 / H1)) ** (np.arange(H1, dtype=np.float64) + 1.0)
SLOPE = _rate * (1.0 + 1e-5)


class Em:
    STEP = {"PE": 1, "DVE": 1, "ACT": 1, "POOL": 1,
            "QSP": 16, "QACT": 16, "QPOOL": 16, "CC": 1}
    CL = ("PE",)

    def __init__(self):
        self.ops = []
        self.lastw = {}
        self.readers = {}

    def op(self, lane, build, r=(), w=(), tag=""):
        i = len(self.ops)
        deps = set()
        for res in r:
            d = self.lastw.get(res)
            if d is not None:
                deps.add(d)
        for res in w:
            d = self.lastw.get(res)
            if d is not None:
                deps.add(d)
            for j in self.readers.get(res, ()):
                deps.add(j)
        deps.discard(i)
        o = {"lane": lane, "build": build, "deps": deps, "i": i,
             "inc": False, "tag": tag}
        self.ops.append(o)
        for res in w:
            self.lastw[res] = i
            self.readers[res] = []
        for res in r:
            self.readers.setdefault(res, []).append(i)
        return o

    def pe(self, build, r=(), w=(), tag=""):
        return self.op("PE", build, r, w, tag)

    def dve(self, build, r=(), w=(), tag=""):
        return self.op("DVE", build, r, w, tag)

    def act(self, build, r=(), w=(), tag=""):
        return self.op("ACT", build, r, w, tag)

    def pool(self, build, r=(), w=(), tag=""):
        return self.op("POOL", build, r, w, tag)

    def emit(self, nc, block):
        NQ = 8
        QL = ("QSP", "QACT", "QPOOL")
        for o in self.ops:
            kept = []
            for d in o["deps"]:
                do = self.ops[d]
                if do["lane"] == o["lane"] and o["lane"] in self.CL:
                    continue
                do["inc"] = True
                kept.append(d)
            o["deps"] = kept
        cnt = {l: 0 for l in self.STEP}
        for o in self.ops:
            if o["inc"]:
                n = cnt[o["lane"]]
                cnt[o["lane"]] += 1
                if o["lane"] in QL:
                    o["sem"] = (o["lane"], n % NQ)
                    o["val"] = 16 * (n // NQ + 1)
                else:
                    o["sem"] = (o["lane"], 0)
                    o["val"] = (n + 1) * self.STEP[o["lane"]]
        for o in self.ops:
            tg = {}
            for d in o["deps"]:
                do = self.ops[d]
                if tg.get(do["sem"], 0) < do["val"]:
                    tg[do["sem"]] = do["val"]
            o["waits"] = tg
        semkeys = set()
        for l in self.STEP:
            if l in QL:
                for i in range(NQ):
                    semkeys.add((l, i))
            else:
                semkeys.add((l, 0))
        sems = {k: nc.semaphore(f"sem_{k[0]}_{k[1]}").__enter__() for k in semkeys}
        self.sems = sems
        bodies = [("tensor", ("PE",)), ("vector", ("DVE",)),
                  ("scalar", ("ACT", "QACT")),
                  ("gpsimd", ("POOL", "QPOOL", "CC")),
                  ("sync", ("QSP",))]
        for bname, lanes in bodies:
            lane_ops = [o for o in self.ops if o["lane"] in lanes]
            if not lane_ops:
                continue

            def body(eng, lane_ops=lane_ops):
                clock = {k: 0 for k in sems}
                for o in lane_ops:
                    for k, v in sorted(o["waits"].items()):
                        if clock[k] < v:
                            eng.wait_ge(sems[k], v)
                            clock[k] = v
                    ins = o["build"](eng)
                    if o["inc"]:
                        ins.then_inc(sems[o["sem"]], self.STEP[o["lane"]])

            getattr(block, bname)(body)

    def stats(self):
        from collections import Counter
        c = Counter(o["lane"] for o in self.ops)
        nw = sum(len(o["waits"]) for o in self.ops if "waits" in o)
        return dict(c), nw


def _u8view(a):
    a = np.ascontiguousarray(a)
    return np.ascontiguousarray(a.view(np.uint8).reshape(*a.shape, 4)[..., 0])


def _bytes_T(a_i32):
    a = np.ascontiguousarray(a_i32)
    b = a.view(np.uint8).reshape(a.shape[0], a.shape[1] * 4)
    return np.ascontiguousarray(b.T)


def _cols(v):
    v = np.asarray(v, np.float32)
    n = v.shape[0] // 128
    return np.ascontiguousarray(v.reshape(n, 128).T)


def host_prep(inputs):
    f32 = np.float32
    g = {}
    for nm, w in (("wqT", inputs["L0_attn_Wq"]), ("wkT", inputs["L0_attn_Wk"]),
                  ("wvT", inputs["L0_attn_Wv"]), ("woT", inputs["L0_attn_Wo"]),
                  ("wigT", inputs["L0_attn_Wig"]), ("wogT", inputs["L0_attn_Wog"])):
        g[nm] = np.ascontiguousarray(np.asarray(w, f32).T)
    g["wgT"] = np.ascontiguousarray(np.asarray(inputs["L0_attn_Wg"], f32).T)
    cw = np.asarray(inputs["L0_attn_conv_w"], f32)
    g["convw"] = np.ascontiguousarray(
        cw.reshape(KT, 128, 4).transpose(1, 0, 2).reshape(128, KT * 4))
    g["convb"] = _cols(inputs["L0_attn_conv_b"])
    g["bigC"] = _cols(inputs["L0_attn_b_ig"])
    g["bogC"] = _cols(inputs["L0_attn_b_og"])
    g["bgC"] = np.ascontiguousarray(np.asarray(inputs["L0_attn_b_g"], f32).reshape(16, 1))
    g["gnw"] = _cols(inputs["L0_attn_gn_w"])
    g["gnb"] = _cols(inputs["L0_attn_gn_b"])
    g["mnw"] = _cols(inputs["L0_attn_mn_w"])
    g["mnb"] = _cols(inputs["L0_attn_mn_b"])
    g["f0gT"] = _u8view(np.asarray(inputs["L0_ffn_wq_gate"]).T)
    g["f0uT"] = _u8view(np.asarray(inputs["L0_ffn_wq_up"]).T)
    g["f0dT"] = _u8view(np.asarray(inputs["L0_ffn_wq_down"]).T)
    g["f0sg"] = _cols(inputs["L0_ffn_s_gate"])
    g["f0su"] = _cols(inputs["L0_ffn_s_up"])
    g["f0sd"] = _cols(inputs["L0_ffn_s_down"])
    g["f0sdrow"] = np.asarray(inputs["L0_ffn_s_down"], f32).reshape(1, D)
    perm_inv = np.concatenate([np.arange(0, D, 2), np.arange(1, D, 2)])
    pih_inv = np.concatenate([np.arange(0, HID, 2), np.arange(1, HID, 2)])
    qkv = np.asarray(inputs["L1_attn_qkv_w"], f32)
    rows = np.concatenate([
        np.concatenate([np.arange(h * 192 + 64 * p, h * 192 + 64 * p + 64)
                        for h in range(H1)]) for p in range(3)])
    g["l1qkvT"] = np.ascontiguousarray(qkv[rows].T[perm_inv])
    g["l1gateT"] = np.ascontiguousarray(
        np.asarray(inputs["L1_attn_gate_w"], f32).T[perm_inv])
    g["l1outT"] = np.ascontiguousarray(np.asarray(inputs["L1_attn_out_w"], f32).T)
    g["l1normw"] = np.asarray(inputs["L1_attn_norm_w"], f32).reshape(1, D)
    gup = np.asarray(inputs["L1_ffn_gate_up_packed"])
    rows_h = np.concatenate([pih_inv, HID + pih_inv])
    g["l1guB"] = _bytes_T(gup[rows_h])
    g["l1guS"] = np.ascontiguousarray(
        np.asarray(inputs["L1_ffn_gate_up_scales"])[:, rows_h])
    g["l1dnB"] = _bytes_T(np.asarray(inputs["L1_ffn_down_packed"]))
    g["l1dnS"] = np.ascontiguousarray(np.asarray(inputs["L1_ffn_down_scales"]))
    g["eye"] = np.eye(128, dtype=BF)
    g["onesrow"] = np.ones((1, PN), BF)
    g["ones1x128"] = np.ones((1, 128), BF)
    g["neg128row"] = np.full((1, 128), -128.0, BF)
    g["onescol"] = np.ones((128, 1), BF)
    ind = np.zeros((128, 16 * 16), BF)
    for h in range(16):
        ind[:, 16 * h + h] = 1
    g["indacc"] = ind
    indb = np.zeros((16, 16 * 128), BF)
    for h in range(16):
        indb[h, 128 * h:128 * h + 128] = 1
    g["indbc"] = indb
    iguS = np.zeros((16, 16 * 128), BF)
    for t in range(16):
        gi = 2 * (t % 8)
        iguS[gi, 128 * t:128 * t + 64] = 1
        iguS[gi + 1, 128 * t + 64:128 * t + 128] = 1
    g["iguS"] = iguS
    idnS = np.zeros((44, 44 * 128), BF)
    for t in range(44):
        gi = 2 * (t % 22)
        idnS[gi, 128 * t:128 * t + 64] = 1
        idnS[gi + 1, 128 * t + 64:128 * t + 128] = 1
    g["idnS"] = idnS
    sl = SLOPE[:, None]
    i_ = np.arange(BS, dtype=np.float64)[None, :]
    qd = np.exp(-sl * (i_ + 1.0))
    kd = np.exp(-sl * (BS - (i_ + 1.0)))
    dd = np.exp(np.where(i_[:, :, None] >= i_[:, None, :],
                         -sl[:, :, None] * (i_[:, :, None] - i_[:, None, :]),
                         -np.inf))
    g["ddT"] = np.ascontiguousarray(
        dd.transpose(0, 2, 1).reshape(H1, 2, 128, BS).astype(BF))
    qdc = np.zeros((128, 64), f32)
    kdc = np.zeros((128, 64), f32)
    for h in range(H1):
        for hf in range(2):
            qdc[:, 2 * h + hf] = qd[h, 128 * hf:128 * hf + 128]
            kdc[:, 2 * h + hf] = kd[h, 128 * hf:128 * hf + 128]
    g["qdc"] = qdc
    g["kdc"] = kdc
    bd = np.exp(-SLOPE * BS)
    bdp = np.zeros((8, 64, D), f32)
    for blk in range(8):
        bdp[blk] = np.repeat((bd ** blk).astype(f32), D1)[None, :]
    g["bdpows"] = bdp
    x = np.asarray(inputs["L0_attn_x"], f32)
    A1 = np.exp(-SLOPE * NT)
    cores = []
    for c in range(NCORES):
        b, qq = divmod(c, 4)
        m = dict(g)
        m["x_tok"] = np.ascontiguousarray(x[b, NT * qq:NT * (qq + 1)])
        m["x_ctx"] = (np.zeros((3, D), f32) if qq == 0 else
                      np.ascontiguousarray(x[b, NT * qq - 3:NT * qq]))
        beta = np.array([1.0 if (i // 4 == b and i % 4 < qq) else 0.0
                         for i in range(NCORES)], f32)
        m["l0chB"] = np.ascontiguousarray(np.broadcast_to(beta[None, :], (128, 8)))
        m["l1chB"] = np.ascontiguousarray(np.broadcast_to(beta[None, :], (64, 8)))
        chM = np.zeros((64, 8 * D), f32)
        for i in range(NCORES):
            vals = (1.0 - beta[i]) + beta[i] * np.repeat(A1, D1)
            chM[:, D * i:D * (i + 1)] = vals[None, :].astype(f32)
        m["l1chM"] = chM
        cores.append(m)
    return cores


def build(stage=99, dbg=()):
    nc = bass.Bass("TRN2", target_bir_lowering=False)
    em = Em()
    ctx = {}

    def I(name, shape, dt):
        ctx[name] = nc.dram_tensor(name, list(shape), dt, kind="ExternalInput").ap()

    def O(name, shape, dt):
        ctx[name] = nc.dram_tensor(name, list(shape), dt, kind="ExternalOutput").ap()

    def S(name, shape, dt):
        ctx[name] = nc.dram_tensor(name, list(shape), dt, kind="Internal").ap()

    I("x_tok", (NT, D), F32); I("x_ctx", (3, D), F32)
    for nm in ("wqT", "wkT", "wvT", "woT", "wigT", "wogT"):
        I(nm, (D, D), F32)
    I("wgT", (D, 16), F32)
    I("convw", (128, KT * 4), F32); I("convb", (128, KT), F32)
    I("bigC", (128, KT), F32); I("bogC", (128, KT), F32); I("bgC", (16, 1), F32)
    for nm in ("gnw", "gnb", "mnw", "mnb"):
        I(nm, (128, KT), F32)
    I("f0gT", (D, HID), U8); I("f0uT", (D, HID), U8); I("f0dT", (HID, D), U8)
    I("f0sg", (128, HT), F32); I("f0su", (128, HT), F32); I("f0sd", (128, KT), F32)
    I("f0sdrow", (1, D), F32)
    I("l1qkvT", (D, 3 * D), F32); I("l1gateT", (D, D), F32); I("l1outT", (D, D), F32)
    I("l1normw", (1, D), F32)
    I("l1guB", (1024, 2 * HID), U8); I("l1guS", (16, 2 * HID), F16)
    I("l1dnB", (2816, D), U8); I("l1dnS", (44, D), F16)
    I("eye", (128, 128), BF16); I("onescol", (128, 1), BF16)
    I("onesrow", (1, PN), BF16); I("ones1x128", (1, 128), BF16)
    I("neg128row", (1, 128), BF16)
    I("indacc", (128, 256), BF16); I("indbc", (16, 2048), BF16)
    I("iguS", (16, 2048), BF16); I("idnS", (44, 44 * 128), BF16)
    I("ddT", (H1, 2, 128, BS), BF16); I("qdc", (128, 64), F32); I("kdc", (128, 64), F32)
    I("bdpows", (8, 64, D), F32)
    I("l0chB", (128, 8), F32); I("l1chB", (64, 8), F32)
    I("l1chM", (64, 8 * D), F32)
    O("y", (NT, D), F32)
    S("xn_sp", (KT, 128, NT), BF16)
    S("xc_sp", (KT, 128, NT), BF16)
    S("attn0_sp", (KT, 128, D), BF16)
    S("q0_sp", (KT, 128, NT), BF16)
    S("og_sp", (KT, 128, NT), BF16)
    S("ffn0_sp", (KT, 128, D), BF16)
    S("x1_sp", (KT, 128, D), F32)
    S("q1_sp", (4, 64, 16384), BF16)
    S("intra_sp", (KT, 128, D), BF16)
    S("ffn1_sp", (KT, 128, D), BF16)
    S("wgu_sp", (16, 128, 2 * HID), BF16)
    S("wdn_sp", (44, 128, D), BF16)
    S("cc0_in", (128, 32), F32); S("cc0_out", (128 * 8, 32), F32)
    S("cc1_in", (64, D), F32); S("cc1_out", (64 * 8, D), F32)
    S("carry1_sp", (64, D), F32)
    S("sl_sp", (8, 64, 2048), BF16)

    sb = {}

    def SB(name, shape, dt):
        sb[name] = nc._state.alloc_sbuf_tensor(name, list(shape), dt)
        return sb[name]

    for nm, shp, dt in (("c_eye", (128, 128), BF16), ("c_indacc", (128, 256), BF16),
                        ("c_indbc", (16, 2048), BF16), ("c_onescol", (128, 1), BF16),
                        ("c_ones1x128", (1, 128), BF16),
                        ("c_convw", (128, KT * 4), F32), ("c_convb", (128, KT), F32),
                        ("c_gnw", (128, KT), F32), ("c_gnb", (128, KT), F32),
                        ("c_mnw", (128, KT), F32), ("c_mnb", (128, KT), F32),
                        ("c_f0sg", (128, HT), F32), ("c_f0su", (128, HT), F32),
                        ("c_f0sd", (128, KT), F32),
                        ("c_qdc", (128, 64), F32), ("c_kdc", (128, 64), F32),
                        ("c_bigC", (128, KT), F32), ("c_bogC", (128, KT), F32),
                        ("c_bgC", (16, 1), F32),
                        ("c_l0chB", (128, 8), F32), ("c_l1chB", (64, 8), F32)):
        SB(nm, shp, dt)

    PS = [nc._state.alloc_psum_tensor(f"ps{i}", [128, PN], F32) for i in range(8)]

    def ps(i):
        return PS[i][:, :]

    def psb(i, n=128):
        return PS[i][:, 0:n // 2].bitcast(BF16)

    for nm in ("c_eye", "c_indacc", "c_indbc", "c_onescol",
               "c_ones1x128", "c_convw",
               "c_convb", "c_gnw", "c_gnb", "c_mnw", "c_mnb", "c_f0sg",
               "c_f0su", "c_f0sd", "c_qdc", "c_kdc",
               "c_bigC", "c_bogC", "c_bgC", "c_l0chB", "c_l1chB"):
        em.op("QSP", lambda e, d=sb[nm], s=ctx[nm[2:]]: e.dma_start(out=d[:], in_=s),
              w=[nm], tag=f"ldc_{nm}")

    SB("AR0", (128, KT * NT), BF16)
    SB("AR2", (128, KT * NT), BF16)
    SB("AR3", (128, 8192), BF16)
    SB("AR4", (128, 12288), BF16)
    SB("TMP", (128, 2048), F32)
    SB("p1_st", (128, 8), F32)
    SB("xc_ctx", (128, KT * 3), BF16)
    SB("grows", (16, NT), F32)
    SB("growsb", (16, 2 * NT), BF16)
    SB("gsc", (16, PN), BF16)
    SB("rowbufb", (1, PN), BF16)
    SB("arow", (1, 16), BF16)
    SB("ccs", (128, 32), F32)
    SB("g0", (128, 256), F32)
    SB("carry0", (128, 16), F32)
    SB("onemb0", (128, 8), F32)
    SB("strow", (1, 1024), F32)
    SB("rowbufb2", (1, PN), BF16)
    SB("f1ss", (128, 8), F32)
    sb["sdb"] = None
    sb["p1_x"] = sb["AR4"][:, 0:4096].bitcast(F32)
    sb["p1_xn_tm"] = sb["AR4"][:, 4096:8192]

    def xn_t(kb):
        return sb["AR0"][:, NT * kb:NT * (kb + 1)]

    def xc_t(kb):
        return sb["AR0"][:, NT * kb:NT * (kb + 1)]

    def xcA_t(kb):
        return sb["AR2"][:, NT * kb:NT * (kb + 1)]

    def kvm_t(h):
        return sb["AR2"][:, NT * h:NT * (h + 1)]

    def sdb_v():
        return sb["AR2"][:, 30720:32768]

    def wslab(mi, buf):
        s = 2 * mi + buf
        return sb["AR4"][:, 2048 * s:2048 * (s + 1)]

    def ln_rows(pfx, nrows, pb, xsl, osl, eps, xkeys=None, okeys=None):
        st = sb["p1_st"]
        r0 = 4 * pb
        if xkeys is None:
            xkeys = [(pfx, "x", pb)]
        if okeys is None:
            okeys = [(pfx, "xn", pb)]
        em.dve(lambda e: e.reduce_sum(st[0:nrows, r0:r0 + 1], xsl, axis=AX),
               r=xkeys, w=[(pfx, "sum", pb)], tag=f"{pfx}_sum")
        em.act(lambda e: e.activation(sb["TMP"][0:nrows, 0:D], xsl, AF.Square),
               r=xkeys, w=["p1sq"], tag=f"{pfx}_sq")
        em.dve(lambda e: e.reduce_sum(st[0:nrows, r0 + 1:r0 + 2],
                                      sb["TMP"][0:nrows, 0:D], axis=AX),
               r=["p1sq"], w=[(pfx, "ssq", pb)], tag=f"{pfx}_ssq")
        em.dve(lambda e: e.tensor_scalar(st[0:nrows, r0:r0 + 1],
                                         st[0:nrows, r0:r0 + 1],
                                         -1.0 / D, None, ALU.mult),
               r=[(pfx, "sum", pb)], w=[(pfx, "sum", pb)], tag=f"{pfx}_negmu")
        em.act(lambda e: e.activation(st[0:nrows, r0 + 2:r0 + 3],
                                      st[0:nrows, r0:r0 + 1], AF.Square),
               r=[(pfx, "sum", pb)], w=[(pfx, "var", pb)], tag=f"{pfx}_mu2")
        em.dve(lambda e: e.scalar_tensor_tensor(
            st[0:nrows, r0 + 2:r0 + 3], st[0:nrows, r0 + 1:r0 + 2], 1.0 / D,
            st[0:nrows, r0 + 2:r0 + 3], ALU.mult, ALU.subtract),
            r=[(pfx, "ssq", pb), (pfx, "var", pb)], w=[(pfx, "var", pb)],
            tag=f"{pfx}_var")
        em.dve(lambda e: e.tensor_scalar(st[0:nrows, r0 + 2:r0 + 3],
                                         st[0:nrows, r0 + 2:r0 + 3],
                                         eps, None, ALU.add),
               r=[(pfx, "var", pb)], w=[(pfx, "var", pb)], tag=f"{pfx}_vare")
        em.act(lambda e: e.activation(st[0:nrows, r0 + 2:r0 + 3],
                                      st[0:nrows, r0 + 2:r0 + 3], AF.Sqrt),
               r=[(pfx, "var", pb)], w=[(pfx, "var", pb)], tag=f"{pfx}_sd")
        em.dve(lambda e: e.reciprocal(st[0:nrows, r0 + 3:r0 + 4],
                                      st[0:nrows, r0 + 2:r0 + 3]),
               r=[(pfx, "var", pb)], w=[(pfx, "rstd", pb)], tag=f"{pfx}_rstd")
        em.dve(lambda e: e.tensor_scalar(osl, xsl, st[0:nrows, r0:r0 + 1],
                                         st[0:nrows, r0 + 3:r0 + 4],
                                         ALU.add, ALU.mult),
               r=xkeys + [(pfx, "sum", pb), (pfx, "rstd", pb)],
               w=okeys, tag=f"{pfx}_norm")

    def p1_emit():
        x = sb["p1_x"]
        xt = sb["p1_xn_tm"]
        for tt in range(KT):
            pb = tt % 2
            em.op("QSP", lambda e, tt=tt, pb=pb: e.dma_start(
                out=sb["p1_x"][:, 0:D],
                in_=ctx["x_tok"][128 * tt:128 * (tt + 1), :]),
                w=[("AR4", 0), ("AR4", 1)], tag=f"p1_ld{tt}")
            ln_rows("p1", 128, pb, x[:, 0:D],
                    xt[:, D * pb:D * (pb + 1)], EPS_LN,
                    xkeys=[("AR4", 0), ("AR4", 1)], okeys=[("AR4", 2 + pb)])
            for kb in range(KT):
                bk = (tt * KT + kb) % 2
                em.pe(lambda e, pb=pb, kb=kb, bk=bk: e.transpose(
                    psb(bk), xt[:, D * pb + 128 * kb:D * pb + 128 * (kb + 1)],
                    sb["c_eye"][:]),
                    r=[("AR4", 2 + pb), "c_eye"], w=[("PS", bk)],
                    tag=f"p1_tr{tt}_{kb}")
                em.dve(lambda e, tt=tt, kb=kb, bk=bk: e.tensor_copy(
                    xn_t(kb)[:, 128 * tt:128 * (tt + 1)],
                    psb(bk)),
                    r=[("PS", bk)], w=[("AR0", kb)], tag=f"p1_ev{tt}_{kb}")
        for kb in range(KT):
            em.op("QSP", lambda e, kb=kb: e.dma_start(
                out=ctx["xn_sp"][kb], in_=xn_t(kb)),
                r=[("AR0", kb)], w=[("xnsp", kb)], tag=f"p1_sp{kb}")
        em.op("QSP", lambda e: e.dma_start(out=sb["p1_x"][0:3, 0:D],
                                           in_=ctx["x_ctx"]),
              w=[("AR4", 0), ("AR4", 1)], tag="p1_ldctx")
        em.dve(lambda e: e.memset(sb["p1_xn_tm"][:, 0:D], 0.0),
               r=[], w=[("AR4", 2)], tag="p1_ctx_z")
        ln_rows("p1c", 3, 0, sb["p1_x"][0:3, 0:D], sb["p1_xn_tm"][0:3, 0:D],
                EPS_LN, xkeys=[("AR4", 0), ("AR4", 1)], okeys=[("AR4", 2)])
        for kb in range(KT):
            bk = kb % 2
            em.pe(lambda e, kb=kb, bk=bk: e.transpose(
                psb(bk), sb["p1_xn_tm"][:, 128 * kb:128 * (kb + 1)],
                sb["c_eye"][:]),
                r=[("AR4", 2), "c_eye"], w=[("PS", bk)], tag=f"p1_ctr{kb}")
            em.dve(lambda e, kb=kb, bk=bk: e.tensor_copy(
                sb["xc_ctx"][:, 3 * kb:3 * (kb + 1)], psb(bk)[:, 0:3]),
                r=[("PS", bk)], w=[("xctx", kb)], tag=f"p1_cev{kb}")

    p1_emit()

    def tmpf(c0, c1):
        ap = sb["TMP"][:, c0:c1]
        keys = [("TMP", s) for s in range(c0 // 512, (c1 + 511) // 512)]
        return ap, keys

    def ar3(c0, c1, dt=None):
        ap = sb["AR3"][:, c0:c1]
        if dt is not None:
            ap = ap.bitcast(dt)
        keys = [("AR3", s) for s in range(c0 // 512, (c1 + 511) // 512)]
        return ap, keys

    def p2_emit():
        cw = sb["c_convw"]
        cb = sb["c_convb"]
        for kb in range(KT):
            cv, cvk = tmpf(0, 2048)
            xnk = xn_t(kb)
            ctx3 = sb["xc_ctx"][:, 3 * kb:3 * (kb + 1)]
            em.dve(lambda e, cv=cv, xnk=xnk, kb=kb: e.tensor_scalar(
                cv, xnk, cw[:, 4 * kb + 3:4 * kb + 4], cb[:, kb:kb + 1],
                ALU.mult, ALU.add),
                r=[("AR0", kb), "c_convw", "c_convb"], w=cvk, tag=f"cv3_{kb}")
            for j, (s0, s1) in ((2, (0, 2047)), (1, (0, 2046)), (0, (0, 2045))):
                off = 3 - j
                em.dve(lambda e, cv=cv, xnk=xnk, kb=kb, j=j, s1=s1, off=off:
                       e.scalar_tensor_tensor(
                           cv[:, off:2048], xnk[:, 0:s1],
                           cw[:, 4 * kb + j:4 * kb + j + 1],
                           cv[:, off:2048], ALU.mult, ALU.add),
                       r=[("AR0", kb), "c_convw"] + cvk, w=cvk, tag=f"cvm{j}_{kb}")
                em.dve(lambda e, cv=cv, ctx3=ctx3, kb=kb, j=j, off=off:
                       e.scalar_tensor_tensor(
                           cv[:, 0:off], ctx3[:, j:3],
                           cw[:, 4 * kb + j:4 * kb + j + 1],
                           cv[:, 0:off], ALU.mult, ALU.add),
                       r=[("xctx", kb), "c_convw"] + cvk, w=cvk, tag=f"cvc{j}_{kb}")
            em.act(lambda e, cv=cv, kb=kb: e.activation(
                xcA_t(kb), cv, AF.Silu),
                r=cvk, w=[("AR2", kb)], tag=f"cvs_{kb}")
            em.op("QSP", lambda e, kb=kb: e.dma_start(
                out=ctx["xc_sp"][kb], in_=xcA_t(kb)),
                r=[("AR2", kb)], w=[("xcsp", kb)], tag=f"xcsp_{kb}")
        em.op("QPOOL", lambda e: e.dma_start(
            out=sb["w_g"][:].rearrange("p (k j) -> p k j", j=16),
            in_=ctx["wgT"].rearrange("(k p) j -> p k j", p=128)),
            w=["w_g"], tag="ld_wg")
        for np_ in range(4):
            for kb in range(KT):
                em.pe(lambda e, np_=np_, kb=kb: e.matmul(
                    ps(np_)[0:16, :], sb["w_g"][:, 16 * kb:16 * (kb + 1)],
                    xcA_t(kb)[:, PN * np_:PN * (np_ + 1)],
                    start=(kb == 0), stop=(kb == 15)),
                    r=["w_g", ("AR2", kb)], w=[("PS", np_)], tag=f"gam_mm{np_}_{kb}")
            em.dve(lambda e, np_=np_: e.tensor_scalar(
                sb["grows"][0:16, PN * np_:PN * (np_ + 1)], ps(np_)[0:16, :],
                sb["c_bgC"][0:16, 0:1], None, ALU.add),
                r=[("PS", np_), "c_bgC"], w=[("grows", np_)], tag=f"gam_ba{np_}")
            em.act(lambda e, np_=np_: e.activation(
                sb["grows"][0:16, PN * np_:PN * (np_ + 1)],
                sb["grows"][0:16, PN * np_:PN * (np_ + 1)], AF.Sigmoid),
                r=[("grows", np_)], w=[("grows", np_)], tag=f"gam_sig{np_}")
            em.dve(lambda e, np_=np_: e.memset(
                sb["TMP"][0:16, 0:PN], 0.0), w=[("TMP", 0)], tag=f"z16_{np_}")
            init = 1.0 if np_ == 0 else sb["growsb"][0:16, NT + PN * np_ - 1:NT + PN * np_]
            em.dve(lambda e, np_=np_, init=init: e.tensor_tensor_scan(
                sb["growsb"][0:16, NT + PN * np_:NT + PN * (np_ + 1)],
                sb["grows"][0:16, PN * np_:PN * (np_ + 1)],
                sb["TMP"][0:16, 0:PN], init, ALU.mult, ALU.add),
                r=[("grows", np_), ("TMP", 0), "pbf"], w=["pbf"], tag=f"p_scan{np_}")
            em.dve(lambda e, np_=np_: e.tensor_copy(
                sb["growsb"][0:16, PN * np_:PN * (np_ + 1)],
                sb["grows"][0:16, PN * np_:PN * (np_ + 1)]),
                r=[("grows", np_)], w=["gbf"], tag=f"gam_bf{np_}")
        qh, qhk = ar3(0, 2048)
        for h in range(16):
            buf = h % 2
            for mi, wn in enumerate(("wqT", "wkT", "wvT")):
                for jj in range(4):
                    em.op("QPOOL", lambda e, wn=wn, h=h, mi=mi, buf=buf, jj=jj:
                          e.dma_start(
                              out=wslab(mi, buf)[:, 512 * jj:512 * (jj + 1)].rearrange(
                                  "p (k j) -> p k j", j=128),
                              in_=ctx[wn][:, 128 * h:128 * (h + 1)].rearrange(
                                  "(k p) j -> p k j", p=128)[:, 4 * jj:4 * (jj + 1), :]),
                          w=[("AR4", 2 * mi + buf)],
                          tag=f"wld{wn}_{h}_{jj}")
            for np_ in range(4):
                npar = np_ % 2
                bq, bk_, bv, bs_ = 4 * npar, 4 * npar + 1, 4 * npar + 2, 4 * npar + 3
                for kt in range(KT):
                    em.pe(lambda e, h=h, buf=buf, kt=kt, np_=np_, bq=bq: e.matmul(
                        ps(bq), wslab(0, buf)[:, 128 * kt:128 * (kt + 1)],
                        xn_t(kt)[:, PN * np_:PN * (np_ + 1)],
                        start=(kt == 0), stop=(kt == 15)),
                        r=[("AR4", 0 + buf), ("AR0", kt)], w=[("PS", bq)],
                        tag=f"q_mm{h}_{np_}_{kt}")
                for kt in range(KT):
                    em.pe(lambda e, h=h, buf=buf, kt=kt, np_=np_, bk_=bk_: e.matmul(
                        ps(bk_), wslab(1, buf)[:, 128 * kt:128 * (kt + 1)],
                        xn_t(kt)[:, PN * np_:PN * (np_ + 1)],
                        start=(kt == 0), stop=(kt == 15)),
                        r=[("AR4", 2 + buf), ("AR0", kt)], w=[("PS", bk_)],
                        tag=f"k_mm{h}_{np_}_{kt}")
                for kt in range(KT):
                    em.pe(lambda e, h=h, buf=buf, kt=kt, np_=np_, bv=bv: e.matmul(
                        ps(bv), wslab(2, buf)[:, 128 * kt:128 * (kt + 1)],
                        xn_t(kt)[:, PN * np_:PN * (np_ + 1)],
                        start=(kt == 0), stop=(kt == 15)),
                        r=[("AR4", 4 + buf), ("AR0", kt)], w=[("PS", bv)],
                        tag=f"v_mm{h}_{np_}_{kt}")
                em.dve(lambda e, qh=qh, np_=np_, bq=bq: e.tensor_copy(
                    qh[:, PN * np_:PN * (np_ + 1)], ps(bq)),
                    r=[("PS", bq)], w=[("qh", np_), ("AR3", np_)],
                    tag=f"q_ev{h}_{np_}")
                sqb, sqbk = ar3(2048 + 512 * npar, 2048 + 512 * (npar + 1))
                em.act(lambda e, sqb=sqb, bk_=bk_: e.activation(
                    sqb, ps(bk_), AF.Square),
                    r=[("PS", bk_)], w=sqbk, tag=f"ksq{h}_{np_}")
                em.pe(lambda e, sqb=sqb, bs_=bs_: e.matmul(
                    ps(bs_)[0:1, :], sb["c_onescol"][:], sqb,
                    start=True, stop=True),
                    r=sqbk + ["c_onescol"], w=[("PS", bs_)], tag=f"kss{h}_{np_}")
                rb = sb["TMP"][0:1, 1024 + 512 * npar:1024 + 512 * (npar + 1)]
                rbk = [("TMP", 2 + npar)]
                em.act(lambda e, bs_=bs_, rb=rb: e.activation(
                    rb, ps(bs_)[0:1, :], AF.Sqrt),
                    r=[("PS", bs_)], w=rbk, tag=f"ksqrt{h}_{np_}")
                em.dve(lambda e, rb=rb: e.reciprocal(rb, rb),
                    r=rbk, w=rbk, tag=f"krcp{h}_{np_}")
                em.dve(lambda e, rb=rb: e.tensor_copy(
                    sb["rowbufb"][0:1, 0:512], rb),
                    r=rbk, w=[("rowb",)], tag=f"krbf{h}_{np_}")
                em.pe(lambda e, bs_=bs_: e.matmul(
                    ps(bs_), sb["c_ones1x128"][:], sb["rowbufb"][0:1, 0:512],
                    start=True, stop=True),
                    r=[("rowb",), "c_ones1x128"], w=[("PS", bs_)],
                    tag=f"kbc{h}_{np_}")
                vs, vsk = ar3(3072 + 512 * npar, 3072 + 512 * (npar + 1))
                em.dve(lambda e, vs=vs, bv=bv: e.tensor_copy(vs, ps(bv)),
                       r=[("PS", bv)], w=vsk, tag=f"v_ev{h}_{np_}")
                m1, m1k = tmpf(512 * npar, 512 * (npar + 1))
                em.dve(lambda e, m1=m1, vs=vs, bk_=bk_: e.tensor_tensor(
                    m1, ps(bk_), vs, ALU.mult),
                    r=[("PS", bk_)] + vsk, w=m1k, tag=f"kv1_{h}_{np_}")
                em.dve(lambda e, m1=m1, h=h, np_=np_, bs_=bs_: e.tensor_tensor(
                    kvm_t(h)[:, PN * np_:PN * (np_ + 1)], m1, ps(bs_), ALU.mult),
                    r=m1k + [("PS", bs_)], w=[("AR2", h)], tag=f"kv2_{h}_{np_}")
            em.op("QSP", lambda e, h=h, qh=qh: e.dma_start(
                out=ctx["q0_sp"][h], in_=qh),
                r=[("qh", 0), ("qh", 1), ("qh", 2), ("qh", 3),
                   ("AR3", 0), ("AR3", 1), ("AR3", 2), ("AR3", 3)],
                w=[("q0sp", h)], tag=f"q_sp{h}")
        for kb in range(KT):
            em.op("QSP", lambda e, kb=kb: e.dma_start(
                out=xc_t(kb), in_=ctx["xc_sp"][kb]),
                r=[("xcsp", kb), ("AR0", kb)], w=[("AR0", kb)], tag=f"xcld_{kb}")
        for h in range(16):
            buf = h % 2
            for jj in range(4):
                em.op("QPOOL", lambda e, h=h, buf=buf, jj=jj: e.dma_start(
                    out=wslab(0, buf)[:, 512 * jj:512 * (jj + 1)].rearrange(
                        "p (k j) -> p k j", j=128),
                    in_=ctx["wigT"][:, 128 * h:128 * (h + 1)].rearrange(
                        "(k p) j -> p k j", p=128)[:, 4 * jj:4 * (jj + 1), :]),
                    w=[("AR4", 2 * 0 + buf)],
                    tag=f"wldig_{h}_{jj}")
            for np_ in range(4):
                npar = np_ % 2
                big, bgb = npar, 2 + npar
                for kt in range(KT):
                    em.pe(lambda e, h=h, buf=buf, kt=kt, np_=np_, big=big: e.matmul(
                        ps(big), wslab(0, buf)[:, 128 * kt:128 * (kt + 1)],
                        xc_t(kt)[:, PN * np_:PN * (np_ + 1)],
                        start=(kt == 0), stop=(kt == 15)),
                        r=[("AR4", 0 + buf), ("AR0", kt)], w=[("PS", big)],
                        tag=f"ig_mm{h}_{np_}_{kt}")
                em.pe(lambda e, h=h, np_=np_, bgb=bgb: e.matmul(
                    ps(bgb), sb["c_indbc"][:, 128 * h:128 * (h + 1)],
                    sb["growsb"][0:16, PN * np_:PN * (np_ + 1)],
                    start=True, stop=False),
                    r=["c_indbc", "gbf"], w=[("PS", bgb)], tag=f"gb_{h}_{np_}")
                em.dve(lambda e, np_=np_: e.tensor_tensor(
                    sb["gsc"][0:16, :],
                    sb["grows"][0:16, PN * np_:PN * (np_ + 1)],
                    sb["growsb"][0:16, PN * np_:PN * (np_ + 1)], ALU.subtract),
                    r=[("grows", np_), "gbf"], w=["gres"], tag=f"gres_{h}_{np_}")
                em.pe(lambda e, h=h, np_=np_, bgb=bgb: e.matmul(
                    ps(bgb), sb["c_indbc"][:, 128 * h:128 * (h + 1)],
                    sb["gsc"][0:16, :],
                    start=False, stop=True),
                    r=["c_indbc", "gres"], w=[("PS", bgb)], tag=f"gbr_{h}_{np_}")
                igs, igsk = tmpf(512 * npar, 512 * (npar + 1))
                em.dve(lambda e, igs=igs, big=big, h=h: e.tensor_scalar(
                    igs, ps(big), sb["c_bigC"][:, h:h + 1], None, ALU.add),
                    r=[("PS", big), "c_bigC"], w=igsk, tag=f"ig_ba{h}_{np_}")
                em.act(lambda e, igs=igs: e.activation(
                    igs, igs, AF.Sigmoid),
                    r=igsk, w=igsk, tag=f"ig_sig{h}_{np_}")
                bt, btk = tmpf(1024 + 512 * npar, 1024 + 512 * (npar + 1))
                em.dve(lambda e, bt=bt, igs=igs, h=h, np_=np_: e.tensor_tensor(
                    bt, igs, kvm_t(h)[:, PN * np_:PN * (np_ + 1)], ALU.mult),
                    r=igsk + [("AR2", h)], w=btk, tag=f"bt_{h}_{np_}")
                init = 0.0 if np_ == 0 else kvm_t(h)[:, PN * np_ - 1:PN * np_]
                em.dve(lambda e, h=h, np_=np_, bt=bt, bgb=bgb, init=init:
                       e.tensor_tensor_scan(
                           kvm_t(h)[:, PN * np_:PN * (np_ + 1)], ps(bgb), bt,
                           init, ALU.mult, ALU.add),
                       r=[("PS", bgb)] + btk + [("AR2", h)], w=[("AR2", h)],
                       tag=f"scan_{h}_{np_}")
            em.dve(lambda e, h=h: e.tensor_copy(
                sb["ccs"][:, h:h + 1], kvm_t(h)[:, NT - 1:NT]),
                r=[("AR2", h)], w=[("ccs", h)], tag=f"bcol_{h}")
        for ot in range(16):
            buf = ot % 2
            ogh, oghk = ar3(4096, 6144)
            for jj in range(4):
                em.op("QPOOL", lambda e, ot=ot, buf=buf, jj=jj: e.dma_start(
                    out=wslab(1, buf)[:, 512 * jj:512 * (jj + 1)].rearrange(
                        "p (k j) -> p k j", j=128),
                    in_=ctx["wogT"][:, 128 * ot:128 * (ot + 1)].rearrange(
                        "(k p) j -> p k j", p=128)[:, 4 * jj:4 * (jj + 1), :]),
                    w=[("AR4", 2 * 1 + buf)],
                    tag=f"wldog_{ot}_{jj}")
            for np_ in range(4):
                bo = 4 + (np_ % 2)
                for kt in range(KT):
                    em.pe(lambda e, ot=ot, buf=buf, kt=kt, np_=np_, bo=bo: e.matmul(
                        ps(bo), wslab(1, buf)[:, 128 * kt:128 * (kt + 1)],
                        xc_t(kt)[:, PN * np_:PN * (np_ + 1)],
                        start=(kt == 0), stop=(kt == 15)),
                        r=[("AR4", 2 + buf), ("AR0", kt)], w=[("PS", bo)],
                        tag=f"og_mm{ot}_{np_}_{kt}")
                ogt, ogtk = tmpf(512 * (np_ % 2), 512 * (np_ % 2 + 1))
                em.dve(lambda e, ogt=ogt, bo=bo, ot=ot: e.tensor_scalar(
                    ogt, ps(bo), sb["c_bogC"][:, ot:ot + 1], None, ALU.add),
                    r=[("PS", bo), "c_bogC"], w=ogtk, tag=f"og_ba{ot}_{np_}")
                em.act(lambda e, ogh=ogh, np_=np_, ogt=ogt: e.activation(
                    ogh[:, PN * np_:PN * (np_ + 1)], ogt, AF.Sigmoid),
                    r=ogtk, w=oghk, tag=f"og_sig{ot}_{np_}")
            em.op("QSP", lambda e, ot=ot, ogh=ogh: e.dma_start(
                out=ctx["og_sp"][ot], in_=ogh),
                r=oghk, w=[("ogsp", ot)], tag=f"og_sp{ot}")
        em.pe(lambda e: e.transpose(
            ps(6)[:, 0:8].bitcast(BF16), sb["growsb"][0:16, 2 * NT - 128:2 * NT],
            sb["c_eye"][0:16, 0:16]),
            r=["pbf", "c_eye"], w=[("PS", 6)], tag="a_tr")
        arow_all, arak = ar3(6144, 6160)
        em.dve(lambda e, arow_all=arow_all: e.tensor_copy(
            arow_all, ps(6)[:, 0:8].bitcast(BF16)),
            r=[("PS", 6)], w=arak, tag="a_ev")
        em.op("QPOOL", lambda e, arow_all=arow_all: e.dma_start(
            out=sb["arow"][0:1, :], in_=arow_all[127:128, :]),
            r=arak, w=["arow"], tag="a_row")
        em.pe(lambda e: e.matmul(
            ps(7)[:, 0:16], sb["c_ones1x128"][:], sb["arow"][0:1, :],
            start=True, stop=True),
            r=["arow", "c_ones1x128"], w=[("PS", 7)], tag="a_bc")
        em.dve(lambda e: e.tensor_copy(sb["ccs"][:, 16:32], ps(7)[:, 0:16]),
               r=[("PS", 7)], w=[("ccs", "a")], tag="a_ev2")
        em.op("QPOOL", lambda e: e.dma_start(out=ctx["cc0_in"], in_=sb["ccs"][:]),
              r=[("ccs", h) for h in range(16)] + [("ccs", "a")], w=["cc0in"],
              tag="cc0_st")
        em.op("CC", lambda e: e.collective_compute(
            "AllGather", ALU.bypass, replica_groups=[list(range(NCORES))],
            ins=[ctx["cc0_in"].opt()], outs=[ctx["cc0_out"].opt()]),
            r=["cc0in"], w=["cc0out"], tag="cc0")
        em.op("QPOOL", lambda e: e.dma_start(
            out=sb["g0"][:].rearrange("p (i c) -> p i c", c=32),
            in_=ctx["cc0_out"].rearrange("(i p) c -> p i c", p=128)),
            r=["cc0out"], w=["g0"], tag="cc0_ld")
        em.dve(lambda e: e.tensor_scalar(
            sb["onemb0"][:], sb["c_l0chB"][:], -1.0, 1.0, ALU.mult, ALU.add),
            r=["c_l0chB"], w=["onemb0"], tag="onemb0")
        em.dve(lambda e: e.memset(sb["carry0"][:], 0.0), w=["carry0"], tag="c0z")
        for i in range(NCORES):
            mi, mik = tmpf(0, 512)
            em.dve(lambda e, i=i, mi=mi: e.tensor_scalar(
                mi[:, 0:16], sb["g0"][:, 32 * i + 16:32 * i + 32],
                sb["c_l0chB"][:, i:i + 1], sb["onemb0"][:, i:i + 1],
                ALU.mult, ALU.add),
                r=["g0", "c_l0chB", "onemb0"], w=mik, tag=f"c0m{i}")
            em.dve(lambda e, mi=mi: e.tensor_tensor(
                sb["carry0"][:], sb["carry0"][:], mi[:, 0:16], ALU.mult),
                r=["carry0"] + mik, w=["carry0"], tag=f"c0x{i}")
            em.dve(lambda e, i=i: e.scalar_tensor_tensor(
                sb["carry0"][:], sb["g0"][:, 32 * i:32 * i + 16],
                sb["c_l0chB"][:, i:i + 1], sb["carry0"][:], ALU.mult, ALU.add),
                r=["g0", "c_l0chB", "carry0"], w=["carry0"], tag=f"c0b{i}")

    if stage >= 2:
        SB("w_g", (128, 256), BF16)
        p2_emit()

    def ar3s(c0, c1, dt=None):
        return ar3(c0, c1, dt)

    def p5_emit():
        for h in range(16):
            qr, qrk = ar3(0, 2048)
            ogr, ogrk = ar3(2048, 4096)
            mcb, mcbk = ar3(4096, 6144)
            sqb3, sqb3k = ar3(6144, 8192)
            em.op("QSP", lambda e, h=h, qr=qr: e.dma_start(
                out=qr, in_=ctx["q0_sp"][h]), r=qrk + [("q0sp", h)], w=qrk,
                tag=f"p5_qld{h}")
            em.op("QSP", lambda e, h=h, ogr=ogr: e.dma_start(
                out=ogr, in_=ctx["og_sp"][h]), r=ogrk + [("ogsp", h)], w=ogrk,
                tag=f"p5_ogld{h}")
            for np_ in range(4):
                sl = slice(PN * np_, PN * (np_ + 1))
                bP, bA, bB = 0, 1, 2
                em.pe(lambda e, h=h, np_=np_: e.matmul(
                    ps(bP), sb["c_indbc"][:, 128 * h:128 * (h + 1)],
                    sb["growsb"][0:16, NT + PN * np_:NT + PN * (np_ + 1)],
                    start=True, stop=True),
                    r=["c_indbc", "pbf"], w=[("PS", bP)], tag=f"p5_pb{h}_{np_}")
                mc = sb["TMP"][:, sl]
                mck = [("TMP", np_)]
                em.dve(lambda e, mc=mc, h=h, sl=sl: e.scalar_tensor_tensor(
                    mc, ps(bP), sb["carry0"][:, h:h + 1], kvm_t(h)[:, sl],
                    ALU.mult, ALU.add),
                    r=[("PS", bP), "carry0", ("AR2", h)], w=mck,
                    tag=f"p5_mc{h}_{np_}")
                em.dve(lambda e, mc=mc, mcb=mcb, sl=sl: e.tensor_copy(
                    mcb[:, sl], mc), r=mck, w=mcbk, tag=f"p5_mcb{h}_{np_}")
                if "p3" in dbg:
                    em.op("QACT", lambda e, mc=mc, h=h, np_=np_: e.dma_start(
                        out=ctx["d_mc"][h][:, PN * np_:PN * (np_ + 1)], in_=mc),
                        r=mck, w=[("dbg", "mc", h, np_)], tag="dbg_mc")
                em.pe(lambda e, mcb=mcb, sl=sl: e.matmul(
                    ps(3)[0:1, :], sb["c_onescol"][:], mcb[:, sl],
                    start=True, stop=True),
                    r=mcbk + ["c_onescol"], w=[("PS", 3)], tag=f"p5_s1{h}_{np_}")
                em.act(lambda e, sqb3=sqb3, mcb=mcb, sl=sl: e.activation(
                    sqb3[:, sl], mcb[:, sl], AF.Square),
                    r=mcbk, w=sqb3k, tag=f"p5_sq{h}_{np_}")
                em.pe(lambda e, sqb3=sqb3, sl=sl: e.matmul(
                    ps(4)[0:1, :], sb["c_onescol"][:], sqb3[:, sl],
                    start=True, stop=True),
                    r=sqb3k + ["c_onescol"], w=[("PS", 4)], tag=f"p5_s2{h}_{np_}")
                em.dve(lambda e: e.tensor_scalar(
                    sb["strow"][0:1, 0:PN], ps(3)[0:1, :], -1.0 / 128, None,
                    ALU.mult), r=[("PS", 3)], w=[("sr", 0)], tag=f"p5_mu{h}_{np_}")
                em.act(lambda e: e.activation(
                    sb["strow"][0:1, PN:2 * PN], sb["strow"][0:1, 0:PN],
                    AF.Square), r=[("sr", 0)], w=[("sr", 1)], tag=f"p5_mu2{h}_{np_}")
                em.dve(lambda e: e.scalar_tensor_tensor(
                    sb["strow"][0:1, PN:2 * PN], ps(4)[0:1, :], 1.0 / 128,
                    sb["strow"][0:1, PN:2 * PN], ALU.mult, ALU.subtract),
                    r=[("PS", 4), ("sr", 1)], w=[("sr", 1)], tag=f"p5_var{h}_{np_}")
                em.dve(lambda e: e.tensor_scalar(
                    sb["strow"][0:1, PN:2 * PN], sb["strow"][0:1, PN:2 * PN],
                    EPS_GN, None, ALU.add),
                    r=[("sr", 1)], w=[("sr", 1)], tag=f"p5_vare{h}_{np_}")
                em.act(lambda e: e.activation(
                    sb["strow"][0:1, PN:2 * PN], sb["strow"][0:1, PN:2 * PN],
                    AF.Sqrt), r=[("sr", 1)], w=[("sr", 1)], tag=f"p5_sd{h}_{np_}")
                em.dve(lambda e: e.reciprocal(
                    sb["strow"][0:1, PN:2 * PN], sb["strow"][0:1, PN:2 * PN]),
                    r=[("sr", 1)], w=[("sr", 1)], tag=f"p5_rs{h}_{np_}")
                em.dve(lambda e: e.tensor_copy(
                    sb["rowbufb"][0:1, 0:PN], sb["strow"][0:1, 0:PN]),
                    r=[("sr", 0)], w=[("rb", 0)], tag=f"p5_mbf{h}_{np_}")
                em.dve(lambda e: e.tensor_copy(
                    sb["rowbufb2"][0:1, 0:PN], sb["strow"][0:1, PN:2 * PN]),
                    r=[("sr", 1)], w=[("rb", 1)], tag=f"p5_rbf{h}_{np_}")
                em.pe(lambda e: e.matmul(
                    ps(bA), sb["c_ones1x128"][:], sb["rowbufb"][0:1, 0:PN],
                    start=True, stop=True),
                    r=[("rb", 0), "c_ones1x128"], w=[("PS", bA)],
                    tag=f"p5_bm{h}_{np_}")
                em.pe(lambda e: e.matmul(
                    ps(bB), sb["c_ones1x128"][:], sb["rowbufb2"][0:1, 0:PN],
                    start=True, stop=True),
                    r=[("rb", 1), "c_ones1x128"], w=[("PS", bB)],
                    tag=f"p5_br{h}_{np_}")
                em.dve(lambda e, mc=mc: e.tensor_tensor(
                    mc, mc, ps(bA), ALU.add),
                    r=mck + [("PS", bA)], w=mck, tag=f"p5_n1{h}_{np_}")
                em.dve(lambda e, mc=mc: e.tensor_tensor(
                    mc, mc, ps(bB), ALU.mult),
                    r=mck + [("PS", bB)], w=mck, tag=f"p5_n2{h}_{np_}")
                em.dve(lambda e, mc=mc, h=h: e.tensor_scalar(
                    mc, mc, sb["c_mnw"][:, h:h + 1], sb["c_mnb"][:, h:h + 1],
                    ALU.mult, ALU.add),
                    r=mck + ["c_mnw", "c_mnb"], w=mck, tag=f"p5_n3{h}_{np_}")
                if "p3" in dbg:
                    em.op("QACT", lambda e, mc=mc, h=h, np_=np_: e.dma_start(
                        out=ctx["d_mn"][h][:, PN * np_:PN * (np_ + 1)], in_=mc),
                        r=mck, w=[("dbg", "mn", h, np_)], tag="dbg_mn")
                em.dve(lambda e, mc=mc, qr=qr, sl=sl: e.tensor_tensor(
                    mc, mc, qr[:, sl], ALU.mult),
                    r=mck + qrk, w=mck, tag=f"p5_nq{h}_{np_}")
                if "p3" in dbg:
                    em.op("QACT", lambda e, mc=mc, h=h, np_=np_: e.dma_start(
                        out=ctx["d_mq"][h][:, PN * np_:PN * (np_ + 1)], in_=mc),
                        r=mck, w=[("dbg", "mq", h, np_)], tag="dbg_mq")
                em.dve(lambda e, mcb=mcb, mc=mc, sl=sl: e.tensor_copy(
                    mcb[:, sl], mc), r=mck, w=mcbk, tag=f"p5_ob{h}_{np_}")
                em.pe(lambda e, mcb=mcb, sl=sl: e.matmul(
                    ps(3)[0:1, :], sb["c_onescol"][:], mcb[:, sl],
                    start=True, stop=True),
                    r=mcbk + ["c_onescol"], w=[("PS", 3)], tag=f"p5_gs1{h}_{np_}")
                em.act(lambda e, sqb3=sqb3, mcb=mcb, sl=sl: e.activation(
                    sqb3[:, sl], mcb[:, sl], AF.Square),
                    r=mcbk, w=sqb3k, tag=f"p5_gsq{h}_{np_}")
                em.pe(lambda e, sqb3=sqb3, sl=sl: e.matmul(
                    ps(4)[0:1, :], sb["c_onescol"][:], sqb3[:, sl],
                    start=True, stop=True),
                    r=sqb3k + ["c_onescol"], w=[("PS", 4)], tag=f"p5_gs2{h}_{np_}")
                em.act(lambda e, sqb3=sqb3, qr=qr, sl=sl: e.activation(
                    sqb3[:, sl], qr[:, sl], AF.Square),
                    r=qrk + [("PS", 4)], w=sqb3k, tag=f"p5_qsq{h}_{np_}")
                em.pe(lambda e, sqb3=sqb3, sl=sl: e.matmul(
                    ps(5)[0:1, :], sb["c_onescol"][:], sqb3[:, sl],
                    start=True, stop=True),
                    r=sqb3k + ["c_onescol"], w=[("PS", 5)], tag=f"p5_qss{h}_{np_}")
                em.dve(lambda e: e.tensor_scalar(
                    sb["strow"][0:1, 0:PN], ps(3)[0:1, :], -1.0 / 128, None,
                    ALU.mult), r=[("PS", 3)], w=[("sr", 0)], tag=f"p5_gmu{h}_{np_}")
                em.act(lambda e: e.activation(
                    sb["strow"][0:1, PN:2 * PN], sb["strow"][0:1, 0:PN],
                    AF.Square), r=[("sr", 0)], w=[("sr", 1)], tag=f"p5_gmu2{h}_{np_}")
                em.dve(lambda e: e.scalar_tensor_tensor(
                    sb["strow"][0:1, PN:2 * PN], ps(4)[0:1, :], 1.0 / 128,
                    sb["strow"][0:1, PN:2 * PN], ALU.mult, ALU.subtract),
                    r=[("PS", 4), ("sr", 1)], w=[("sr", 1)], tag=f"p5_gvar{h}_{np_}")
                em.dve(lambda e: e.scalar_tensor_tensor(
                    sb["strow"][0:1, PN:2 * PN], ps(5)[0:1, :], EPS_GN,
                    sb["strow"][0:1, PN:2 * PN], ALU.mult, ALU.add),
                    r=[("PS", 5), ("sr", 1)], w=[("sr", 1)], tag=f"p5_geps{h}_{np_}")
                em.act(lambda e: e.activation(
                    sb["strow"][0:1, PN:2 * PN], sb["strow"][0:1, PN:2 * PN],
                    AF.Sqrt), r=[("sr", 1)], w=[("sr", 1)], tag=f"p5_gsd{h}_{np_}")
                em.dve(lambda e: e.reciprocal(
                    sb["strow"][0:1, PN:2 * PN], sb["strow"][0:1, PN:2 * PN]),
                    r=[("sr", 1)], w=[("sr", 1)], tag=f"p5_grs{h}_{np_}")
                em.dve(lambda e: e.tensor_copy(
                    sb["rowbufb"][0:1, 0:PN], sb["strow"][0:1, 0:PN]),
                    r=[("sr", 0)], w=[("rb", 0)], tag=f"p5_gmbf{h}_{np_}")
                em.dve(lambda e: e.tensor_copy(
                    sb["rowbufb2"][0:1, 0:PN], sb["strow"][0:1, PN:2 * PN]),
                    r=[("sr", 1)], w=[("rb", 1)], tag=f"p5_grbf{h}_{np_}")
                em.pe(lambda e: e.matmul(
                    ps(bA), sb["c_ones1x128"][:], sb["rowbufb"][0:1, 0:PN],
                    start=True, stop=True),
                    r=[("rb", 0), "c_ones1x128"], w=[("PS", bA)],
                    tag=f"p5_gbm{h}_{np_}")
                em.pe(lambda e: e.matmul(
                    ps(bB), sb["c_ones1x128"][:], sb["rowbufb2"][0:1, 0:PN],
                    start=True, stop=True),
                    r=[("rb", 1), "c_ones1x128"], w=[("PS", bB)],
                    tag=f"p5_gbr{h}_{np_}")
                em.dve(lambda e, mc=mc: e.tensor_tensor(
                    mc, mc, ps(bA), ALU.add),
                    r=mck + [("PS", bA)], w=mck, tag=f"p5_g1{h}_{np_}")
                em.dve(lambda e, mc=mc: e.tensor_tensor(
                    mc, mc, ps(bB), ALU.mult),
                    r=mck + [("PS", bB)], w=mck, tag=f"p5_g2{h}_{np_}")
                em.dve(lambda e, mc=mc, h=h: e.tensor_scalar(
                    mc, mc, sb["c_gnw"][:, h:h + 1], sb["c_gnb"][:, h:h + 1],
                    ALU.mult, ALU.add),
                    r=mck + ["c_gnw", "c_gnb"], w=mck, tag=f"p5_g3{h}_{np_}")
                if "p3" in dbg:
                    em.op("QACT", lambda e, mc=mc, h=h, np_=np_: e.dma_start(
                        out=ctx["d_gn"][h][:, PN * np_:PN * (np_ + 1)], in_=mc),
                        r=mck, w=[("dbg", "gn", h, np_)], tag="dbg_gn")
                em.dve(lambda e, mc=mc, ogr=ogr, h=h, sl=sl: e.tensor_tensor(
                    kvm_t(h)[:, sl], mc, ogr[:, sl], ALU.mult),
                    r=mck + ogrk, w=[("AR2", h)], tag=f"p5_go{h}_{np_}")
        for npo in range(4):
            for g2 in range(2):
                for kh in range(8):
                    khg = g2 * 8 + kh
                    wsl5, wsl5k = ar3(4096 * g2 + 512 * kh, 4096 * g2 + 512 * (kh + 1))
                    em.op("QPOOL", lambda e, khg=khg, npo=npo, wsl5=wsl5:
                          e.dma_start(
                              out=wsl5,
                              in_=ctx["woT"][128 * khg:128 * (khg + 1),
                                             PN * npo:PN * (npo + 1)]),
                          r=wsl5k, w=wsl5k, tag=f"p5_wo{npo}_{khg}")
            for tt in range(KT):
                bk = 6 + tt % 2
                for khg in range(16):
                    wsl5, wsl5k = ar3(4096 * (khg // 8) + 512 * (khg % 8),
                                      4096 * (khg // 8) + 512 * (khg % 8 + 1))
                    em.pe(lambda e, khg=khg, tt=tt, bk=bk, wsl5=wsl5: e.matmul(
                        ps(bk), kvm_t(khg)[:, 128 * tt:128 * (tt + 1)], wsl5,
                        start=(khg == 0), stop=(khg == 15)),
                        r=[("AR2", khg)] + wsl5k, w=[("PS", bk)],
                        tag=f"p5_womm{npo}_{tt}_{khg}")
                a0, a0k = tmpf(512 * (tt % 4), 512 * (tt % 4 + 1))
                em.dve(lambda e, a0=a0, bk=bk: e.tensor_copy(
                    a0.bitcast(BF16)[:, 0:PN], ps(bk)),
                    r=[("PS", bk)], w=a0k, tag=f"p5_a0ev{npo}_{tt}")
                em.op("QSP", lambda e, a0=a0, tt=tt, npo=npo: e.dma_start(
                    out=ctx["attn0_sp"][tt][:, PN * npo:PN * (npo + 1)],
                    in_=a0.bitcast(BF16)[:, 0:PN]),
                    r=a0k, w=[("a0sp", tt, npo)], tag=f"p5_a0sp{npo}_{tt}")

    def p4_emit():
        em.dve(lambda e: e.memset(sb["gsc"][0:1, :], 1.0),
               w=["onesr"], tag="p4_onesr")
        for kb in range(KT):
            em.op("QSP", lambda e, kb=kb: e.dma_start(
                out=xn_t(kb), in_=ctx["xn_sp"][kb]),
                r=[("xnsp", kb), ("AR0", kb)], w=[("AR0", kb)], tag=f"p4_xnld{kb}")
        for np_ in range(4):
            em.op("QPOOL", lambda e, np_=np_: e.dma_start(
                out=sb["rowbufb2"][0:1, :],
                in_=ctx["f0sdrow"][0:1, PN * np_:PN * (np_ + 1)]),
                r=[("rb", 1)], w=[("rb", 1)], tag=f"p4_sdrow{np_}")
            em.pe(lambda e, np_=np_: e.matmul(
                ps(5), sb["c_ones1x128"][:], sb["rowbufb2"][0:1, :],
                start=True, stop=True),
                r=[("rb", 1), "c_ones1x128"], w=[("PS", 5)], tag=f"p4_sdb{np_}")
            em.dve(lambda e, np_=np_: e.tensor_copy(
                sdb_v()[:, PN * np_:PN * (np_ + 1)], ps(5)),
                r=[("PS", 5)], w=["sdb", ("AR2", 15)], tag=f"p4_sdbe{np_}")
        for ch in range(4):
            tsl = slice(PN * ch, PN * (ch + 1))
            for kt in range(KT):
                em.pe(lambda e, kt=kt, tsl=tsl: e.matmul(
                    ps(5)[0:1, :], sb["c_onescol"][:], xn_t(kt)[:, tsl],
                    start=(kt == 0), stop=(kt == 15)),
                    r=[("AR0", kt), "c_onescol"], w=[("PS", 5)], tag=f"p4_S{ch}_{kt}")
            em.dve(lambda e: e.tensor_scalar(
                sb["strow"][0:1, 0:PN], ps(5)[0:1, :], -128.0, None, ALU.mult),
                r=[("PS", 5)], w=[("sr", 0)], tag=f"p4_Sm{ch}")
            em.dve(lambda e: e.tensor_copy(
                sb["rowbufb"][0:1, 0:PN], sb["strow"][0:1, 0:PN]),
                r=[("sr", 0)], w=[("rb", 0)], tag=f"p4_Sbf{ch}")
            em.dve(lambda e: e.tensor_tensor(
                sb["strow"][0:1, PN:2 * PN], sb["strow"][0:1, 0:PN],
                sb["rowbufb"][0:1, 0:PN], ALU.subtract),
                r=[("sr", 0), ("rb", 0)], w=[("sr", 1)], tag=f"p4_Sr{ch}")
            em.dve(lambda e: e.tensor_copy(
                sb["rowbufb2"][0:1, 0:PN], sb["strow"][0:1, PN:2 * PN]),
                r=[("sr", 1), ("rb", 1)], w=[("rb", 1)], tag=f"p4_Srb{ch}")
            for ot in range(HT):
                buf = ot % 2
                for mi, wn in ((0, "f0gT"), (1, "f0uT")):
                    em.op("QPOOL", lambda e, wn=wn, ot=ot, mi=mi, buf=buf:
                          e.dma_start(
                              out=wslab(mi, buf)[:].rearrange(
                                  "p (k j) -> p k j", j=128),
                              in_=ctx[wn][:, 128 * ot:128 * (ot + 1)].rearrange(
                                  "(k p) j -> p k j", p=128)),
                          w=[("AR4", 2 * mi + buf)], tag=f"p4_w{wn}{ch}_{ot}")
                bg, bu = buf, 2 + buf
                for kt in range(KT):
                    em.pe(lambda e, kt=kt, buf=buf, tsl=tsl, bg=bg: e.matmul(
                        ps(bg), wslab(0, buf)[:, 128 * kt:128 * (kt + 1)],
                        xn_t(kt)[:, tsl], start=(kt == 0), stop=False),
                        r=[("AR4", 0 + buf), ("AR0", kt)], w=[("PS", bg)],
                        tag=f"p4_gmm{ch}_{ot}_{kt}")
                em.pe(lambda e, bg=bg: e.matmul(
                    ps(bg), sb["c_ones1x128"][:], sb["rowbufb"][0:1, 0:PN],
                    start=False, stop=False),
                    r=[("rb", 0), "c_ones1x128"], w=[("PS", bg)],
                    tag=f"p4_gc1{ch}_{ot}")
                em.pe(lambda e, bg=bg: e.matmul(
                    ps(bg), sb["c_ones1x128"][:], sb["rowbufb2"][0:1, 0:PN],
                    start=False, stop=True),
                    r=[("rb", 1), "c_ones1x128"], w=[("PS", bg)],
                    tag=f"p4_gc2{ch}_{ot}")
                for kt in range(KT):
                    em.pe(lambda e, kt=kt, buf=buf, tsl=tsl, bu=bu: e.matmul(
                        ps(bu), wslab(1, buf)[:, 128 * kt:128 * (kt + 1)],
                        xn_t(kt)[:, tsl], start=(kt == 0), stop=False),
                        r=[("AR4", 2 + buf), ("AR0", kt)], w=[("PS", bu)],
                        tag=f"p4_umm{ch}_{ot}_{kt}")
                em.pe(lambda e, bu=bu: e.matmul(
                    ps(bu), sb["c_ones1x128"][:], sb["rowbufb"][0:1, 0:PN],
                    start=False, stop=False),
                    r=[("rb", 0), "c_ones1x128"], w=[("PS", bu)],
                    tag=f"p4_uc1{ch}_{ot}")
                em.pe(lambda e, bu=bu: e.matmul(
                    ps(bu), sb["c_ones1x128"][:], sb["rowbufb2"][0:1, 0:PN],
                    start=False, stop=True),
                    r=[("rb", 1), "c_ones1x128"], w=[("PS", bu)],
                    tag=f"p4_uc2{ch}_{ot}")
                gt, gtk = tmpf(512 * buf, 512 * (buf + 1))
                em.act(lambda e, gt=gt, bg=bg, ot=ot: e.activation(
                    gt, ps(bg), AF.Silu, scale=sb["c_f0sg"][:, ot:ot + 1]),
                    r=[("PS", bg), "c_f0sg"], w=gtk, tag=f"p4_silu{ch}_{ot}")
                em.dve(lambda e, gt=gt, bu=bu, ot=ot: e.scalar_tensor_tensor(
                    sb["AR2"][:, 512 * ot:512 * (ot + 1)], ps(bu),
                    sb["c_f0su"][:, ot:ot + 1], gt, ALU.mult, ALU.mult),
                    r=[("PS", bu), "c_f0su"] + gtk,
                    w=[("h0", ot), ("AR2", ot // 4)],
                    tag=f"p4_h{ch}_{ot}")
            for kh in range(HT):
                em.pe(lambda e, kh=kh: e.matmul(
                    ps(5)[0:1, :], sb["c_onescol"][:],
                    sb["AR2"][:, 512 * kh:512 * (kh + 1)],
                    start=(kh == 0), stop=(kh == HT - 1)),
                    r=[("h0", kh), "c_onescol"], w=[("PS", 5)], tag=f"p4_hs{ch}_{kh}")
            em.dve(lambda e: e.tensor_scalar(
                sb["strow"][0:1, 0:PN], ps(5)[0:1, :], -128.0, None, ALU.mult),
                r=[("PS", 5)], w=[("sr", 0)], tag=f"p4_hsm{ch}")
            em.dve(lambda e: e.tensor_copy(
                sb["rowbufb"][0:1, 0:PN], sb["strow"][0:1, 0:PN]),
                r=[("sr", 0)], w=[("rb", 0)], tag=f"p4_hsbf{ch}")
            em.dve(lambda e: e.tensor_tensor(
                sb["strow"][0:1, PN:2 * PN], sb["strow"][0:1, 0:PN],
                sb["rowbufb"][0:1, 0:PN], ALU.subtract),
                r=[("sr", 0), ("rb", 0)], w=[("sr", 1)], tag=f"p4_hsr{ch}")
            em.dve(lambda e: e.tensor_copy(
                sb["rowbufb2"][0:1, 0:PN], sb["strow"][0:1, PN:2 * PN]),
                r=[("sr", 1), ("rb", 1)], w=[("rb", 1)], tag=f"p4_hsrb{ch}")
            for npo in range(4):
                for grp in range(3):
                    khs = range(16 * grp, min(16 * (grp + 1), HT))
                    for kh in khs:
                        dsl, dslk = ar3(512 * (kh % 16), 512 * (kh % 16 + 1))
                        em.op("QPOOL", lambda e, kh=kh, npo=npo, dsl=dsl:
                              e.dma_start(
                                  out=dsl,
                                  in_=ctx["f0dT"][128 * kh:128 * (kh + 1),
                                                  PN * npo:PN * (npo + 1)]),
                              r=dslk, w=dslk, tag=f"p4_dw{ch}_{npo}_{kh}")
                    for tt4 in range(4):
                        bd = 4 + tt4
                        if grp == 0:
                            em.pe(lambda e, tt4=tt4, bd=bd: e.matmul(
                                ps(bd), sb["rowbufb"][0:1, 128 * tt4:128 * (tt4 + 1)],
                                sb["gsc"][0:1, :], start=True, stop=False),
                                r=[("rb", 0), "onesr"], w=[("PS", bd)],
                                tag=f"p4_dc1{ch}_{npo}_{tt4}")
                            em.pe(lambda e, tt4=tt4, bd=bd: e.matmul(
                                ps(bd), sb["rowbufb2"][0:1, 128 * tt4:128 * (tt4 + 1)],
                                sb["gsc"][0:1, :], start=False, stop=False),
                                r=[("rb", 1), "onesr"], w=[("PS", bd)],
                                tag=f"p4_dc2{ch}_{npo}_{tt4}")
                        for kh in khs:
                            dsl, dslk = ar3(512 * (kh % 16), 512 * (kh % 16 + 1))
                            em.pe(lambda e, kh=kh, tt4=tt4, dsl=dsl, bd=bd:
                                  e.matmul(
                                      ps(bd), sb["AR2"][:, 512 * kh + 128 * tt4:
                                                        512 * kh + 128 * (tt4 + 1)],
                                      dsl, start=False, stop=(kh == HT - 1)),
                                  r=[("h0", kh)] + dslk, w=[("PS", bd)],
                                  tag=f"p4_dmm{ch}_{npo}_{grp}_{tt4}_{kh}")
                for tt4 in range(4):
                    bd = 4 + tt4
                    f0, f0k = tmpf(512 + 512 * (tt4 % 2), 1024 + 512 * (tt4 % 2))
                    em.dve(lambda e, f0=f0, bd=bd, npo=npo: e.tensor_tensor(
                        f0.bitcast(BF16)[:, 0:PN], ps(bd),
                        sdb_v()[:, PN * npo:PN * (npo + 1)], ALU.mult),
                        r=[("PS", bd), "sdb"], w=f0k, tag=f"p4_fev{ch}_{npo}_{tt4}")
                    tt_g = ch * 4 + tt4
                    em.op("QSP", lambda e, f0=f0, tt_g=tt_g, npo=npo: e.dma_start(
                        out=ctx["ffn0_sp"][tt_g][:, PN * npo:PN * (npo + 1)],
                        in_=f0.bitcast(BF16)[:, 0:PN]),
                        r=f0k, w=[("f0sp", tt_g, npo)], tag=f"p4_fsp{ch}_{npo}_{tt4}")

    def p6_emit():
        for tt in range(KT):
            pb = tt % 2
            a0l = sb["AR4"][:, 8192:10240]
            f0l = sb["AR4"][:, 10240:12288]
            em.op("QSP", lambda e, tt=tt: e.dma_start(
                out=sb["p1_x"][:, 0:D],
                in_=ctx["x_tok"][128 * tt:128 * (tt + 1), :]),
                w=[("AR4", 0), ("AR4", 1)], tag=f"p6_ldx{tt}")
            em.op("QSP", lambda e, tt=tt, a0l=a0l: e.dma_start(
                out=a0l, in_=ctx["attn0_sp"][tt]),
                r=[("a0sp", tt, npo) for npo in range(4)] + [("AR4", 4)],
                w=[("AR4", 4)], tag=f"p6_lda{tt}")
            em.op("QSP", lambda e, tt=tt, f0l=f0l: e.dma_start(
                out=f0l, in_=ctx["ffn0_sp"][tt]),
                r=[("f0sp", tt, npo) for npo in range(4)] + [("AR4", 5)],
                w=[("AR4", 5)], tag=f"p6_ldf{tt}")
            x1t = sb["p1_x"][:, 0:D]
            x1k = [("AR4", 0), ("AR4", 1)]
            em.dve(lambda e, a0l=a0l, x1t=x1t: e.tensor_tensor(
                x1t, x1t, a0l, ALU.add),
                r=x1k + [("AR4", 4)], w=x1k, tag=f"p6_add1{tt}")
            em.dve(lambda e, f0l=f0l, x1t=x1t: e.tensor_tensor(
                x1t, x1t, f0l, ALU.add),
                r=x1k + [("AR4", 5)], w=x1k, tag=f"p6_add2{tt}")
            em.op("QSP", lambda e, tt=tt, x1t=x1t: e.dma_start(
                out=ctx["x1_sp"][tt], in_=x1t),
                r=x1k, w=[("x1sp", tt)], tag=f"p6_x1sp{tt}")
            ln_rows("p6", 128, pb, x1t, sb["p1_xn_tm"][:, D * pb:D * (pb + 1)],
                    EPS_LN, xkeys=x1k, okeys=[("AR4", 2 + pb)])
            for kb in range(KT):
                bkp = (tt * KT + kb) % 2
                if kb < 8:
                    srccol = 256 * kb
                else:
                    srccol = 256 * (kb - 8) + 1
                em.pe(lambda e, pb=pb, kb=kb, bkp=bkp, srccol=srccol: e.transpose(
                    psb(bkp),
                    sb["p1_xn_tm"][:, D * pb + srccol:D * pb + srccol + 255:2],
                    sb["c_eye"][:]),
                    r=[("AR4", 2 + pb), "c_eye"], w=[("PS", bkp)],
                    tag=f"p6_tr{tt}_{kb}")
                em.dve(lambda e, tt=tt, kb=kb, bkp=bkp: e.tensor_copy(
                    xn_t(kb)[:, 128 * tt:128 * (tt + 1)], psb(bkp)),
                    r=[("PS", bkp)], w=[("AR0", kb)], tag=f"p6_ev{tt}_{kb}")

    if stage >= 3 and "p3" in dbg:
        O("d_mc", (KT, 128, NT), F32)
        O("d_mn", (KT, 128, NT), F32)
        O("d_mq", (KT, 128, NT), F32)
        O("d_gn", (KT, 128, NT), F32)
    def p0b_emit():
        def dq_matrix(srcB, srcS, nrows_m, ncc, ccw, ntile_half, out_sp, tag0):
            for cc in range(ncc):
                c0 = ccw * cc
                cw = min(ccw, ctx[srcB].shape[1] - c0)
                for m in range(nrows_m):
                    B = sb["AR4"][:, 0:1024].bitcast(U8)[:, 0:cw]
                    ulo = sb["AR4"][:, 1024:2048].bitcast(U8)[:, 0:cw]
                    slo = sb["AR4"][:, 2048:3072].bitcast(U8)[:, 0:cw]
                    tb = sb["AR4"][:, 3072:4096].bitcast(U8)[:, 0:cw]
                    uhi = sb["AR4"][:, 4096:5120].bitcast(U8)[:, 0:cw]
                    shi = sb["AR4"][:, 5120:6144].bitcast(U8)[:, 0:cw]
                    kB = [("AR4", 0)]; kulo = [("AR4", 0)]
                    kslo = [("AR4", 1)]; kt = [("AR4", 1)]
                    kuhi = [("AR4", 2)]; kshi = [("AR4", 2)]
                    em.op("QSP", lambda e, B=B, m=m, c0=c0, cw=cw, srcB=srcB:
                          e.dma_start(out=B, in_=ctx[srcB][128 * m:128 * (m + 1),
                                                          c0:c0 + cw]),
                          r=kB, w=kB, tag=f"{tag0}_B{cc}_{m}")
                    em.dve(lambda e, ulo=ulo, B=B: e.tensor_scalar(
                        ulo, B, 7, None, ALU.bitwise_and),
                        r=kB, w=kulo, tag=f"{tag0}_ulo")
                    em.dve(lambda e, slo=slo, B=B: e.tensor_scalar(
                        slo, B, 8, None, ALU.bitwise_and),
                        r=kB, w=kslo, tag=f"{tag0}_slo")
                    em.dve(lambda e, tb=tb, B=B: e.tensor_scalar(
                        tb, B, 4, None, ALU.logical_shift_right),
                        r=kB, w=kt, tag=f"{tag0}_t")
                    em.dve(lambda e, uhi=uhi, tb=tb: e.tensor_scalar(
                        uhi, tb, 7, None, ALU.bitwise_and),
                        r=kt, w=kuhi, tag=f"{tag0}_uhi")
                    em.dve(lambda e, shi=shi, tb=tb: e.tensor_scalar(
                        shi, tb, 8, None, ALU.bitwise_and),
                        r=kt, w=kshi, tag=f"{tag0}_shi")
                    for half, (uu, ss, kuu, kss) in enumerate(
                            ((ulo, slo, kulo, kslo), (uhi, shi, kuhi, kshi))):
                        t_out = m if half == 0 else ntile_half + m
                        scb = sb["AR4"][:, 6144:10240].bitcast(F32)[:, 0:cw]
                        kscb = [("AR4", 3), ("AR45", 0), ("AR45", 1),
                                ("AR45", 2), ("AR45", 3)]
                        gi0 = 2 * (t_out % ntile_half)
                        em.op("QPOOL", lambda e, scb=scb, gi0=gi0, c0=c0, cw=cw,
                              srcS=srcS: e.dma_start(
                                  out=scb[0:64, :],
                                  in_=ctx[srcS][gi0:gi0 + 1, c0:c0 + cw]
                                  .broadcast_to([64, cw])),
                              r=kscb, w=kscb, tag=f"{tag0}_sc0")
                        em.op("QPOOL", lambda e, scb=scb, gi0=gi0, c0=c0, cw=cw,
                              srcS=srcS: e.dma_start(
                                  out=scb[64:128, :],
                                  in_=ctx[srcS][gi0 + 1:gi0 + 2, c0:c0 + cw]
                                  .broadcast_to([64, cw])),
                              r=kscb, w=kscb, tag=f"{tag0}_sc1")
                        for pc in range(0, cw, 1024):
                            pw = min(1024, cw - pc)
                            fI = sb["TMP"][:, 0:1024].bitcast(I32)[:, 0:pw]
                            fF = sb["TMP"][:, 0:1024][:, 0:pw]
                            bF = sb["TMP"][:, 1024:2048][:, 0:pw]
                            gF = sb["AR4"][:, 10240:12288].bitcast(F32)[:, 0:pw]
                            kf = [("TMP", 0), ("TMP", 1)]
                            kb2 = [("TMP", 2), ("TMP", 3)]
                            kg = [("AR45", 4), ("AR45", 5), ("AR45", 6),
                                  ("AR45", 7)]
                            wO = sb["AR3"][:, 0:2048][:, pc:pc + pw]
                            kw = [("AR3", 0), ("AR3", 1), ("AR3", 2), ("AR3", 3)]
                            em.dve(lambda e, fI=fI, uu=uu, pc=pc, pw=pw:
                                    e.tensor_copy(fI, uu[:, pc:pc + pw]),
                                    r=kuu, w=kf, tag=f"{tag0}_i32")
                            em.dve(lambda e, fI=fI: e.tensor_scalar(
                                fI, fI, 0x400000, 0x3F000000, ALU.mult, ALU.add),
                                r=kf, w=kf, tag=f"{tag0}_bits")
                            em.dve(lambda e, bF=bF, fF=fF: e.tensor_scalar(
                                bF, fF, 2.0, 1.0, ALU.mult, ALU.subtract),
                                r=kf, w=kb2, tag=f"{tag0}_2f1")
                            em.dve(lambda e, bF=bF: e.tensor_scalar(
                                bF, bF, 0.0, None, ALU.max),
                                r=kb2, w=kb2, tag=f"{tag0}_rl")
                            em.dve(lambda e, bF=bF, fF=fF: e.tensor_tensor(
                                bF, bF, fF, ALU.min),
                                r=kb2 + kf, w=kb2, tag=f"{tag0}_min")
                            em.act(lambda e, gF=gF, ss=ss, pc=pc, pw=pw:
                                   e.activation(gF, ss[:, pc:pc + pw], AF.Identity,
                                                scale=-0.25),
                                   r=kss, w=kg, tag=f"{tag0}_sg1")
                            em.dve(lambda e, gF=gF: e.tensor_scalar(
                                gF, gF, 1.0, None, ALU.add),
                                r=kg, w=kg, tag=f"{tag0}_sg2")
                            em.dve(lambda e, bF=bF, gF=gF: e.tensor_tensor(
                                bF, bF, gF, ALU.mult),
                                r=kb2 + kg, w=kb2, tag=f"{tag0}_vs")
                            em.dve(lambda e, wO=wO, bF=bF, scb=scb, pc=pc, pw=pw:
                                   e.tensor_tensor(wO, bF, scb[:, pc:pc + pw],
                                                   ALU.mult),
                                   r=kb2 + kscb, w=kw, tag=f"{tag0}_ws")
                        em.op("QSP", lambda e, t_out=t_out, c0=c0, cw=cw,
                              out_sp=out_sp: e.dma_start(
                                  out=ctx[out_sp][t_out][:, c0:c0 + cw],
                                  in_=sb["AR3"][:, 0:2048][:, 0:cw]),
                              r=kw, w=[("wsp", out_sp, t_out, cc)],
                              tag=f"{tag0}_wsp")
        dq_matrix("l1guB", "l1guS", 8, 6, 2048, 8, "wgu_sp", "dqg")
        dq_matrix("l1dnB", "l1dnS", 22, 1, 2048, 22, "wdn_sp", "dqd")

    def q_cv():
        return sb["AR2"][0:64, 8192:24576]

    def intra_cv(tt4):
        return sb["AR2"][:, 24576 + 2048 * tt4:24576 + 2048 * (tt4 + 1)]

    def vtm_t(tt4):
        return sb["AR2"][:, 2048 * tt4:2048 * (tt4 + 1)]

    S_ap = None

    def p7a_emit():
        S = sb["AR3"][0:64, 2048:6144].bitcast(F32)
        kS = [("AR3", 4), ("AR3", 5), ("AR3", 6), ("AR3", 7)]
        k2 = sb["AR3"][0:64, 0:1024]
        kk2 = [("AR3", 0), ("AR3", 1)]
        awb = sb["AR3"][:, 1024:1536]
        kaw = [("AR3", 2)]
        ktm = sb["AR3"][:, 1536:1600]
        kkt = [("AR3", 3)]
        em.dve(lambda e, S=S: e.memset(S[:, :], 0.0), w=kS, tag="p7_Sz")
        for chk in range(4):
            tsl = slice(PN * chk, PN * (chk + 1))
            for vs4 in range(4):
                for kt in range(KT):
                    vb = kt % 2
                    vsl = sb["AR4"][:, 8192 + 512 * vb:8192 + 512 * (vb + 1)]
                    kvsl = [("AR45", vb)]
                    em.op("QPOOL", lambda e, vsl=vsl, kt=kt, vs4=vs4: e.dma_start(
                        out=vsl, in_=ctx["l1qkvT"][128 * kt:128 * (kt + 1),
                                                   2 * D + 512 * vs4:
                                                   2 * D + 512 * (vs4 + 1)]),
                        r=kvsl, w=kvsl, tag=f"p7_wv{chk}_{vs4}_{kt}")
                    for tt in range(4):
                        em.pe(lambda e, vsl=vsl, kt=kt, tt=tt, chk=chk: e.matmul(
                            ps(tt), xn_t(kt)[:, PN * chk + 128 * tt:
                                             PN * chk + 128 * (tt + 1)],
                            vsl, start=(kt == 0), stop=(kt == 15)),
                            r=kvsl + [("AR0", kt)], w=[("PS", tt)],
                            tag=f"p7_vmm{chk}_{vs4}_{kt}_{tt}")
                for tt in range(4):
                    em.act(lambda e, tt=tt, vs4=vs4: e.activation(
                        vtm_t(tt)[:, 512 * vs4:512 * (vs4 + 1)], ps(tt), AF.Silu),
                        r=[("PS", tt)], w=[("AR2v", tt, vs4)],
                        tag=f"p7_vev{chk}_{vs4}_{tt}")
            for hq in range(H1):
                buf = hq % 2
                qsl = sb["AR4"][:, 2048 * buf:2048 * (buf + 1)][:, 0:1024]
                kqsl = [("AR4", buf)]
                em.op("QPOOL", lambda e, qsl=qsl, hq=hq: e.dma_start(
                    out=qsl.rearrange("p (k j) -> p k j", j=64),
                    in_=ctx["l1qkvT"][:, 64 * hq:64 * (hq + 1)].rearrange(
                        "(k p) j -> p k j", p=128)),
                    r=kqsl, w=kqsl, tag=f"p7_wq{chk}_{hq}")
                bq = 4 + buf
                for kt in range(KT):
                    em.pe(lambda e, qsl=qsl, kt=kt, tsl=tsl, bq=bq: e.matmul(
                        ps(bq)[0:64, :], qsl[:, 64 * kt:64 * (kt + 1)],
                        xn_t(kt)[:, tsl], start=(kt == 0), stop=(kt == 15)),
                        r=kqsl + [("AR0", kt)], w=[("PS", bq)],
                        tag=f"p7_qmm{chk}_{hq}_{kt}")
                em.act(lambda e, hq=hq, bq=bq: e.activation(
                    q_cv()[:, 512 * hq:512 * (hq + 1)], ps(bq)[0:64, :], AF.Silu),
                    r=[("PS", bq)], w=[("AR2q", hq)], tag=f"p7_qev{chk}_{hq}")
            em.op("QSP", lambda e, chk=chk: e.dma_start(
                out=ctx["q1_sp"][chk], in_=q_cv()),
                r=[("AR2q", hq) for hq in range(H1)], w=[("q1sp", chk)],
                tag=f"p7_qsp{chk}")
            for h in range(H1):
                buf = h % 2
                ksl = sb["AR4"][:, 4096 + 2048 * buf:4096 + 2048 * buf + 1024]
                kksl = [("AR4", 2 + buf)]
                em.op("QPOOL", lambda e, ksl=ksl, h=h: e.dma_start(
                    out=ksl.rearrange("p (k j) -> p k j", j=64),
                    in_=ctx["l1qkvT"][:, D + 64 * h:D + 64 * (h + 1)]
                    .rearrange("(k p) j -> p k j", p=128)),
                    r=kksl, w=kksl, tag=f"p7_wk{chk}_{h}")
                bk2 = 6
                for kt in range(KT):
                    em.pe(lambda e, ksl=ksl, kt=kt, tsl=tsl, bk2=bk2: e.matmul(
                        ps(bk2)[0:64, :], ksl[:, 64 * kt:64 * (kt + 1)],
                        xn_t(kt)[:, tsl], start=(kt == 0), stop=(kt == 15)),
                        r=kksl + [("AR0", kt)], w=[("PS", bk2)],
                        tag=f"p7_kmm{chk}_{h}_{kt}")
                em.act(lambda e, bk2=bk2: e.activation(
                    k2[:, 0:512], ps(bk2)[0:64, :], AF.Silu),
                    r=[("PS", bk2)], w=[("AR3", 0)], tag=f"p7_k2{chk}_{h}")
                ddb = sb["AR4"][:, 8192 + 1024 + 512 * buf:
                                8192 + 1024 + 512 * (buf + 1)]
                kdd = [("AR45", 2 + buf)]
                em.op("QPOOL", lambda e, ddb=ddb, h=h: e.dma_start(
                    out=ddb.rearrange("p (a c) -> p a c", c=BS),
                    in_=ctx["ddT"][h].rearrange("a p c -> p a c")),
                    r=kdd, w=kdd, tag=f"p7_dd{chk}_{h}")
                for blk in range(2):
                    em.op("QPOOL", lambda e, h=h, blk=blk, chk=chk, S=S:
                          e.dma_start(
                              out=ctx["sl_sp"][2 * chk + blk][:, 64 * h:
                                                              64 * (h + 1)],
                              in_=S[:, 64 * h:64 * (h + 1)]),
                          r=kS, w=[("slsp", 2 * chk + blk, h)],
                          tag=f"p7_slog{chk}_{h}_{blk}")
                    for tjh in range(2):
                        em.pe(lambda e, tjh=tjh, blk=blk, h=h: e.matmul(
                            ps(7)[:, 256 * tjh:256 * (tjh + 1)],
                            k2[:, 256 * blk + 128 * tjh:
                               256 * blk + 128 * (tjh + 1)],
                            q_cv()[:, 512 * h + 256 * blk:
                                   512 * h + 256 * (blk + 1)],
                            start=True, stop=True),
                            r=[("AR3", 0), ("AR2q", h)], w=[("PS", 7)],
                            tag=f"p7_aw{chk}_{h}_{blk}_{tjh}")
                    em.dve(lambda e, ddb=ddb: e.tensor_tensor(
                        awb, ps(7), ddb, ALU.mult),
                        r=[("PS", 7)] + kdd, w=kaw, tag=f"p7_awb{chk}_{h}_{blk}")
                    bi = 4 + blk
                    for tih in range(2):
                        for tjh in range(2):
                            em.pe(lambda e, tih=tih, tjh=tjh, blk=blk, h=h,
                                  bi=bi: e.matmul(
                                      ps(bi)[:, 64 * tih:64 * (tih + 1)],
                                      awb[:, 256 * tjh + 128 * tih:
                                          256 * tjh + 128 * (tih + 1)],
                                      vtm_t(2 * blk + tjh)[:, 64 * h:
                                                           64 * (h + 1)],
                                      start=(tjh == 0), stop=(tjh == 1)),
                                  r=kaw + [("AR2v", 2 * blk + tjh, h // 8)],
                                  w=[("PS", bi)],
                                  tag=f"p7_in{chk}_{h}_{blk}_{tih}_{tjh}")
                    for tih in range(2):
                        em.dve(lambda e, tih=tih, blk=blk, h=h, bi=bi:
                               e.tensor_copy(
                                   intra_cv(2 * blk + tih)[:, 64 * h:
                                                           64 * (h + 1)],
                                   ps(bi)[:, 64 * tih:64 * (tih + 1)]),
                               r=[("PS", bi)], w=[("AR2i", 2 * blk + tih)],
                               tag=f"p7_iev{chk}_{h}_{blk}_{tih}")
                    for tjh in range(2):
                        em.pe(lambda e, blk=blk, tjh=tjh: e.transpose(
                            ps(7)[:, 0:32].bitcast(BF16),
                            k2[:, 256 * blk + 128 * tjh:
                               256 * blk + 128 * (tjh + 1)],
                            sb["c_eye"][0:64, 0:64]),
                            r=[("AR3", 0), "c_eye"], w=[("PS", 7)],
                            tag=f"p7_ktr{chk}_{h}_{blk}_{tjh}")
                        em.dve(lambda e, h=h, tjh=tjh: e.tensor_scalar(
                            ktm, ps(7)[:, 0:32].bitcast(BF16),
                            sb["c_kdc"][:, 2 * h + tjh:2 * h + tjh + 1], None,
                            ALU.mult),
                            r=[("PS", 7), "c_kdc"], w=kkt,
                            tag=f"p7_ktd{chk}_{h}_{blk}_{tjh}")
                        em.pe(lambda e, h=h, blk=blk, tjh=tjh: e.matmul(
                            ps(6)[0:64, 0:64], ktm,
                            vtm_t(2 * blk + tjh)[:, 64 * h:64 * (h + 1)],
                            start=(tjh == 0), stop=(tjh == 1)),
                            r=kkt + [("AR2v", 2 * blk + tjh, h // 8)],
                            w=[("PS", 6)], tag=f"p7_st{chk}_{h}_{blk}_{tjh}")
                    em.dve(lambda e, h=h, S=S: e.tensor_scalar(
                        S[:, 64 * h:64 * (h + 1)], S[:, 64 * h:64 * (h + 1)],
                        float(np.exp(-SLOPE[h] * BS)), None, ALU.mult),
                        r=kS, w=kS, tag=f"p7_sbd{chk}_{h}_{blk}")
                    em.dve(lambda e, h=h, S=S: e.tensor_tensor(
                        S[:, 64 * h:64 * (h + 1)], S[:, 64 * h:64 * (h + 1)],
                        ps(6)[0:64, 0:64], ALU.add),
                        r=kS + [("PS", 6)], w=kS,
                        tag=f"p7_sadd{chk}_{h}_{blk}")
            for tt in range(4):
                em.op("QSP", lambda e, tt=tt, chk=chk: e.dma_start(
                    out=ctx["intra_sp"][4 * chk + tt], in_=intra_cv(tt)),
                    r=[("AR2i", tt)],
                    w=[("isp", 4 * chk + tt)], tag=f"p7_isp{chk}_{tt}")
        em.op("QPOOL", lambda e, S=S: e.dma_start(out=ctx["cc1_in"], in_=S),
              r=kS, w=["cc1in"], tag="cc1_st")
        em.op("CC", lambda e: e.collective_compute(
            "AllGather", ALU.bypass, replica_groups=[list(range(NCORES))],
            ins=[ctx["cc1_in"].opt()], outs=[ctx["cc1_out"].opt()]),
            r=["cc1in"], w=["cc1out"], tag="cc1")

    def p7b_emit():
        carry = sb["AR3"][0:64, 0:4096].bitcast(F32)
        kc = [("AR3", s) for s in range(8)]
        gsl = sb["AR3"][0:64, 4096:8192].bitcast(F32)
        kg = [("AR3", s) for s in range(8, 16)]
        Mi = sb["TMP"][0:64, :]
        kM = [("TMP", s) for s in range(4)]
        em.dve(lambda e, carry=carry: e.memset(carry[:, :], 0.0),
               w=kc, tag="p7b_cz")
        for i in range(NCORES):
            em.op("QSP", lambda e, i=i, gsl=gsl: e.dma_start(
                out=gsl, in_=ctx["cc1_out"][64 * i:64 * (i + 1), :]),
                r=kg + ["cc1out"], w=kg, tag=f"p7b_g{i}")
            em.op("QSP", lambda e, i=i, Mi=Mi: e.dma_start(
                out=Mi, in_=ctx["l1chM"][:, D * i:D * (i + 1)]),
                r=kM, w=kM, tag=f"p7b_m{i}")
            em.dve(lambda e, carry=carry, Mi=Mi: e.tensor_tensor(
                carry[:, :], carry[:, :], Mi, ALU.mult),
                r=kc + kM, w=kc, tag=f"p7b_x{i}")
            em.dve(lambda e, i=i, carry=carry, gsl=gsl: e.scalar_tensor_tensor(
                carry[:, :], gsl[:, :], sb["c_l1chB"][:, i:i + 1],
                carry[:, :], ALU.mult, ALU.add),
                r=kg + kc + ["c_l1chB"], w=kc, tag=f"p7b_b{i}")
        em.op("QPOOL", lambda e, carry=carry: e.dma_start(
            out=ctx["carry1_sp"], in_=carry[:, :]),
            r=kc, w=["c1sp"], tag="p7b_csp")

    def p8_emit(nodown=False, onechunk=False, dbg=dbg):
        for chk in range(1 if onechunk else 4):
            tsl = slice(PN * chk, PN * (chk + 1))
            for j in range(HT):
                buf = j % 2
                for mi, ot in ((0, j), (1, HT + j)):
                    for jj in range(4):
                        em.op("QPOOL", lambda e, mi=mi, ot=ot, buf=buf, jj=jj:
                              e.dma_start(
                                  out=wslab(mi, buf)[:, 512 * jj:512 * (jj + 1)]
                                  .rearrange("p (k j) -> p k j", j=128),
                                  in_=ctx["wgu_sp"][4 * jj:4 * (jj + 1), :,
                                                    128 * ot:128 * (ot + 1)]
                                  .rearrange("t p j -> p t j")),
                              r=[("wsp", "wgu_sp", t, (128 * ot) // 2048)
                                 for t in range(4 * jj, 4 * jj + 4)],
                              w=[("AR4", 2 * mi + buf)],
                              tag=f"p8_w{chk}_{mi}_{j}_{jj}")
                bg, bu = buf, 2 + buf
                for kt in range(KT):
                    em.pe(lambda e, kt=kt, buf=buf, tsl=tsl, bg=bg: e.matmul(
                        ps(bg), wslab(0, buf)[:, 128 * kt:128 * (kt + 1)],
                        xn_t(kt)[:, tsl], start=(kt == 0), stop=(kt == 15)),
                        r=[("AR4", 0 + buf), ("AR0", kt)], w=[("PS", bg)],
                        tag=f"p8_gmm{chk}_{j}_{kt}")
                for kt in range(KT):
                    em.pe(lambda e, kt=kt, buf=buf, tsl=tsl, bu=bu: e.matmul(
                        ps(bu), wslab(1, buf)[:, 128 * kt:128 * (kt + 1)],
                        xn_t(kt)[:, tsl], start=(kt == 0), stop=(kt == 15)),
                        r=[("AR4", 2 + buf), ("AR0", kt)], w=[("PS", bu)],
                        tag=f"p8_umm{chk}_{j}_{kt}")
                gt8, gt8k = tmpf(512 * buf, 512 * (buf + 1))
                em.act(lambda e, gt8=gt8, bg=bg: e.activation(
                    gt8, ps(bg), AF.Silu),
                    r=[("PS", bg)], w=gt8k, tag=f"p8_silu{chk}_{j}")
                if 512 * j < 8192:
                    ovl = [("AR2v", (512 * j) // 2048, ((512 * j) // 512) % 4)]
                else:
                    ovl = [("AR2q", (512 * j - 8192) // 512)]
                em.dve(lambda e, gt8=gt8, bu=bu, j=j: e.tensor_tensor(
                    sb["AR2"][:, 512 * j:512 * (j + 1)], ps(bu), gt8, ALU.mult),
                    r=[("PS", bu)] + gt8k, w=[("h0", j)] + ovl,
                    tag=f"p8_h{chk}_{j}")
            if nodown:
                continue
            em.dve(lambda e: e.memset(sb["f1ss"][:, 0:4], 0.0),
                   w=["f1ss"], tag=f"p8_ssz{chk}")
            for npo in range(4):
                for grp in range(3):
                    khs = range(16 * grp, min(16 * (grp + 1), HT))
                    for kh in khs:
                        dsl, dslk = ar3(512 * (kh % 16), 512 * (kh % 16 + 1))
                        em.op("QPOOL", lambda e, kh=kh, npo=npo, dsl=dsl:
                              e.dma_start(
                                  out=dsl,
                                  in_=ctx["wdn_sp"][kh][:, PN * npo:
                                                        PN * (npo + 1)]),
                              r=dslk + [("wsp", "wdn_sp", kh, 0)], w=dslk,
                              tag=f"p8_dw{chk}_{npo}_{kh}")
                    for tt4 in range(4):
                        bd_ = 4 + tt4
                        for kh in khs:
                            dsl, dslk = ar3(512 * (kh % 16), 512 * (kh % 16 + 1))
                            em.pe(lambda e, kh=kh, tt4=tt4, dsl=dsl, bd_=bd_:
                                  e.matmul(
                                      ps(bd_), sb["AR2"][:, 512 * kh + 128 * tt4:
                                                         512 * kh + 128 * (tt4 + 1)],
                                      dsl, start=(kh == 0), stop=(kh == HT - 1)),
                                  r=[("h0", kh)] + dslk, w=[("PS", bd_)],
                                  tag=f"p8_dmm{chk}_{npo}_{grp}_{tt4}_{kh}")
                for tt4 in range(4):
                    bd_ = 4 + tt4
                    yb = sb["AR4"][:, 2048 * tt4:2048 * (tt4 + 1)]
                    em.dve(lambda e, yb=yb, bd_=bd_, npo=npo: e.tensor_copy(
                        yb[:, PN * npo:PN * (npo + 1)], ps(bd_)),
                        r=[("PS", bd_)], w=[("AR4", tt4)], tag=f"p8_yev{chk}_{npo}_{tt4}")
                    sq8, sq8k = tmpf(0, 512)
                    em.act(lambda e, sq8=sq8, yb=yb, npo=npo: e.activation(
                        sq8, yb[:, PN * npo:PN * (npo + 1)], AF.Square),
                        r=[("AR4", tt4)], w=sq8k, tag=f"p8_sq{chk}_{npo}_{tt4}")
                    em.dve(lambda e, sq8=sq8, tt4=tt4: e.reduce_sum(
                        sb["f1ss"][:, 4 + tt4:5 + tt4], sq8, axis=AX),
                        r=sq8k, w=[("f1r", tt4)], tag=f"p8_red{chk}_{npo}_{tt4}")
                    em.dve(lambda e, tt4=tt4: e.tensor_tensor(
                        sb["f1ss"][:, tt4:tt4 + 1], sb["f1ss"][:, tt4:tt4 + 1],
                        sb["f1ss"][:, 4 + tt4:5 + tt4], ALU.add),
                        r=[("f1r", tt4), "f1ss"], w=["f1ss"],
                        tag=f"p8_acc{chk}_{npo}_{tt4}")
            if "p8nr" in dbg:
                for tt4 in range(4):
                    yb = sb["AR4"][:, 2048 * tt4:2048 * (tt4 + 1)]
                    em.op("QSP", lambda e, yb=yb, tt4=tt4, chk=chk: e.dma_start(
                        out=ctx["ffn1_sp"][4 * chk + tt4], in_=yb),
                        r=[("AR4", tt4)], w=[("f1sp", 4 * chk + tt4)],
                        tag=f"p8_fspX{chk}_{tt4}")
                continue
            for tt4 in range(4):
                em.dve(lambda e, tt4=tt4: e.tensor_scalar(
                    sb["f1ss"][:, 4 + tt4:5 + tt4], sb["f1ss"][:, tt4:tt4 + 1],
                    1.0 / D, EPS_RMS, ALU.mult, ALU.add),
                    r=["f1ss"], w=[("f1r", tt4)], tag=f"p8_ms{chk}_{tt4}")
                em.act(lambda e, tt4=tt4: e.activation(
                    sb["f1ss"][:, 4 + tt4:5 + tt4],
                    sb["f1ss"][:, 4 + tt4:5 + tt4], AF.Sqrt),
                    r=[("f1r", tt4)], w=[("f1r", tt4)], tag=f"p8_sqr{chk}_{tt4}")
                em.dve(lambda e, tt4=tt4: e.reciprocal(
                    sb["f1ss"][:, 4 + tt4:5 + tt4],
                    sb["f1ss"][:, 4 + tt4:5 + tt4]),
                    r=[("f1r", tt4)], w=[("f1r", tt4)], tag=f"p8_rc{chk}_{tt4}")
                yb = sb["AR4"][:, 2048 * tt4:2048 * (tt4 + 1)]
                ybs = sb["TMP"][:, 0:1024].bitcast(BF16)
                kys = [("TMP", 0), ("TMP", 1)]
                em.dve(lambda e, yb=yb, ybs=ybs, tt4=tt4: e.tensor_scalar(
                    ybs, yb, sb["f1ss"][:, 4 + tt4:5 + tt4], None, ALU.mult),
                    r=[("AR4", tt4), ("f1r", tt4)] + kys, w=kys,
                    tag=f"p8_scl{chk}_{tt4}")
                em.op("QSP", lambda e, ybs=ybs, tt4=tt4, chk=chk: e.dma_start(
                    out=ctx["ffn1_sp"][4 * chk + tt4], in_=ybs),
                    r=kys, w=[("f1sp", 4 * chk + tt4)],
                    tag=f"p8_fsp{chk}_{tt4}")

    def p7c_emit():
        nwb = sb["AR2"][:, 26624:28672]
        knw = [("AR2i", 1)]
        for np_ in range(4):
            em.op("QPOOL", lambda e, np_=np_: e.dma_start(
                out=sb["rowbufb2"][0:1, 0:PN],
                in_=ctx["l1normw"][0:1, PN * np_:PN * (np_ + 1)]),
                r=[("rb", 1)], w=[("rb", 1)], tag=f"p7c_nwl{np_}")
            em.pe(lambda e, np_=np_: e.matmul(
                ps(7), sb["c_ones1x128"][:], sb["rowbufb2"][0:1, 0:PN],
                start=True, stop=True),
                r=[("rb", 1), "c_ones1x128"], w=[("PS", 7)], tag=f"p7c_nwb{np_}")
            em.dve(lambda e, np_=np_, nwb=nwb: e.tensor_copy(
                nwb[:, PN * np_:PN * (np_ + 1)], ps(7)),
                r=[("PS", 7)], w=knw, tag=f"p7c_nwe{np_}")
        carry = sb["AR3"][0:64, 0:4096].bitcast(F32)
        kc = [("AR3", s) for s in range(8)]
        em.op("QSP", lambda e, carry=carry: e.dma_start(
            out=carry[:, :], in_=ctx["carry1_sp"]),
            r=kc + ["c1sp"], w=kc, tag="p7c_cld")
        Slg = sb["AR3"][0:64, 4096:6144]
        kSlg = [("AR3", 8), ("AR3", 9), ("AR3", 10), ("AR3", 11)]
        Strue = sb["AR3"][0:64, 6144:8192]
        kSt = [("AR3", 12), ("AR3", 13), ("AR3", 14), ("AR3", 15)]
        bdp = sb["TMP"][0:64, :]
        kbdp = [("TMP", s) for s in range(4)]
        for chk in range(4):
            em.op("QSP", lambda e, chk=chk: e.dma_start(
                out=q_cv(), in_=ctx["q1_sp"][chk]),
                r=[("q1sp", chk)] + [("AR2q", hq) for hq in range(H1)],
                w=[("AR2q", hq) for hq in range(H1)], tag=f"p7c_qld{chk}")
            for blk in range(2):
                gblk = 2 * chk + blk
                em.op("QSP", lambda e, gblk=gblk, Slg=Slg: e.dma_start(
                    out=Slg, in_=ctx["sl_sp"][gblk]),
                    r=kSlg + [("slsp", gblk, h) for h in range(H1)],
                    w=kSlg, tag=f"p7c_sld{gblk}")
                em.op("QSP", lambda e, gblk=gblk, bdp=bdp: e.dma_start(
                    out=bdp, in_=ctx["bdpows"][gblk]),
                    r=kbdp, w=kbdp, tag=f"p7c_bld{gblk}")
                em.dve(lambda e, bdp=bdp, carry=carry: e.tensor_tensor(
                    sb["TMP"][0:64, :], bdp, carry[:, :], ALU.mult),
                    r=kbdp + kc, w=kbdp, tag=f"p7c_bc{gblk}")
                em.dve(lambda e, Strue=Strue, Slg=Slg: e.tensor_tensor(
                    Strue, sb["TMP"][0:64, :], Slg, ALU.add),
                    r=kbdp + kSlg, w=kSt, tag=f"p7c_st{gblk}")
                for tih in range(2):
                    tt4 = 2 * blk + tih
                    att = sb["AR2"][:, 2048 * tt4:2048 * (tt4 + 1)]
                    katt = [("AR2v", tt4, v) for v in range(4)]
                    em.op("QSP", lambda e, chk=chk, tt4=tt4, att=att: e.dma_start(
                        out=att, in_=ctx["intra_sp"][4 * chk + tt4]),
                        r=katt + [("isp", 4 * chk + tt4)], w=katt,
                        tag=f"p7c_ild{chk}_{tt4}")
                for h in range(H1):
                    for tih in range(2):
                        tt4 = 2 * blk + tih
                        att = sb["AR2"][:, 2048 * tt4:2048 * (tt4 + 1)]
                        katt = [("AR2v", tt4, v) for v in range(4)]
                        em.pe(lambda e, h=h, blk=blk, tih=tih: e.matmul(
                            ps(tih)[:, 0:64],
                            q_cv()[:, 512 * h + 256 * blk + 128 * tih:
                                   512 * h + 256 * blk + 128 * (tih + 1)],
                            Strue[:, 64 * h:64 * (h + 1)],
                            start=True, stop=True),
                            r=[("AR2q", h)] + kSt, w=[("PS", tih)],
                            tag=f"p7c_int{chk}_{h}_{blk}_{tih}")
                        em.dve(lambda e, h=h, tih=tih, att=att: e.scalar_tensor_tensor(
                            att[:, 64 * h:64 * (h + 1)], ps(tih)[:, 0:64],
                            sb["c_qdc"][:, 2 * h + tih:2 * h + tih + 1],
                            att[:, 64 * h:64 * (h + 1)], ALU.mult, ALU.add),
                            r=[("PS", tih), "c_qdc"] + katt, w=katt,
                            tag=f"p7c_ia{chk}_{h}_{blk}_{tih}")
            for tt4 in range(4):
                gtt = 4 * chk + tt4
                att = sb["AR2"][:, 2048 * tt4:2048 * (tt4 + 1)]
                katt = [("AR2v", tt4, v) for v in range(4)]
                sq7, sq7k = tmpf(0, 2048)
                em.act(lambda e, att=att, sq7=sq7: e.activation(
                    sq7, att, AF.Square), r=katt, w=sq7k, tag=f"p7c_sq{gtt}")
                em.dve(lambda e, sq7=sq7: e.reduce_sum(
                    sb["f1ss"][:, 0:1], sq7, axis=AX),
                    r=sq7k, w=["f1ss"], tag=f"p7c_red{gtt}")
                em.dve(lambda e: e.tensor_scalar(
                    sb["f1ss"][:, 1:2], sb["f1ss"][:, 0:1], 1.0 / D, EPS_GN,
                    ALU.mult, ALU.add),
                    r=["f1ss"], w=["f1ss"], tag=f"p7c_ms{gtt}")
                em.act(lambda e: e.activation(
                    sb["f1ss"][:, 1:2], sb["f1ss"][:, 1:2], AF.Sqrt),
                    r=["f1ss"], w=["f1ss"], tag=f"p7c_sqr{gtt}")
                em.dve(lambda e: e.reciprocal(
                    sb["f1ss"][:, 1:2], sb["f1ss"][:, 1:2]),
                    r=["f1ss"], w=["f1ss"], tag=f"p7c_rc{gtt}")
                em.dve(lambda e, att=att: e.tensor_scalar(
                    att, att, sb["f1ss"][:, 1:2], None, ALU.mult),
                    r=katt + ["f1ss"], w=katt, tag=f"p7c_rs{gtt}")
                em.dve(lambda e, att=att, nwb=nwb: e.tensor_tensor(
                    att, att, nwb, ALU.mult),
                    r=katt + knw, w=katt, tag=f"p7c_nw{gtt}")
                for npg in range(4):
                    for kt in range(KT):
                        gb2 = kt % 2
                        gsl7 = sb["AR4"][:, 8192 + 512 * gb2:
                                         8192 + 512 * (gb2 + 1)]
                        kgsl = [("AR45", gb2)]
                        em.op("QPOOL", lambda e, npg=npg, kt=kt, gsl7=gsl7:
                              e.dma_start(
                                  out=gsl7,
                                  in_=ctx["l1gateT"][128 * kt:128 * (kt + 1),
                                                     PN * npg:PN * (npg + 1)]),
                              r=kgsl, w=kgsl, tag=f"p7c_wg{gtt}_{npg}_{kt}")
                        em.pe(lambda e, kt=kt, gtt=gtt, npg=npg, gsl7=gsl7:
                              e.matmul(
                                  ps(4 + npg % 2), xn_t(kt)[:, 128 * gtt:
                                                            128 * (gtt + 1)],
                                  gsl7, start=(kt == 0), stop=(kt == 15)),
                              r=kgsl + [("AR0", kt)], w=[("PS", 4 + npg % 2)],
                              tag=f"p7c_gmm{gtt}_{npg}_{kt}")
                    gte = sb["AR4"][:, 10240 + 512 * (npg % 2):
                                    10240 + 512 * (npg % 2 + 1)]
                    em.act(lambda e, gte=gte, npg=npg: e.activation(
                        gte, ps(4 + npg % 2), AF.Sigmoid),
                        r=[("PS", 4 + npg % 2)], w=[("AR45", 4 + npg % 2)],
                        tag=f"p7c_gsig{gtt}_{npg}")
                    em.dve(lambda e, att=att, gte=gte, npg=npg: e.tensor_tensor(
                        att[:, PN * npg:PN * (npg + 1)],
                        att[:, PN * npg:PN * (npg + 1)], gte, ALU.mult),
                        r=katt + [("AR45", 4 + npg % 2)], w=katt,
                        tag=f"p7c_gm{gtt}_{npg}")
                gtf = sb["AR2"][:, 24576:26624]
                kgtf = [("AR2i", 0)]
                for kb in range(KT):
                    bkt = kb % 2
                    em.pe(lambda e, kb=kb, att=att, bkt=bkt: e.transpose(
                        psb(bkt), att[:, 128 * kb:128 * (kb + 1)], sb["c_eye"][:]),
                        r=katt + ["c_eye"], w=[("PS", bkt)], tag=f"p7c_tr{gtt}_{kb}")
                    em.dve(lambda e, kb=kb, gtf=gtf, bkt=bkt: e.tensor_copy(
                        gtf[:, 128 * kb:128 * (kb + 1)], psb(bkt)),
                        r=[("PS", bkt)], w=kgtf, tag=f"p7c_te{gtt}_{kb}")
                for npo in range(4):
                    for kt in range(KT):
                        ob2 = kt % 2
                        osl = sb["AR4"][:, 8192 + 512 * ob2:
                                        8192 + 512 * (ob2 + 1)]
                        kosl = [("AR45", ob2)]
                        em.op("QPOOL", lambda e, npo=npo, kt=kt, osl=osl:
                              e.dma_start(
                                  out=osl,
                                  in_=ctx["l1outT"][128 * kt:128 * (kt + 1),
                                                    PN * npo:PN * (npo + 1)]),
                              r=kosl, w=kosl, tag=f"p7c_wo{gtt}_{npo}_{kt}")
                        em.pe(lambda e, kt=kt, gtt=gtt, npo=npo, osl=osl,
                              gtf=gtf: e.matmul(
                                  ps(6 + npo % 2), gtf[:, 128 * kt:128 * (kt + 1)],
                                  osl, start=(kt == 0), stop=(kt == 15)),
                              r=kosl + kgtf, w=[("PS", 6 + npo % 2)],
                              tag=f"p7c_omm{gtt}_{npo}_{kt}")
                    x1l = sb["AR2"][:, 28672:30720]
                    kx1 = [("AR2i", 2)]
                    f1l = sb["AR2"][:, 30720:32768]
                    kf1 = [("AR2i", 3)]
                    if npo == 0:
                        em.op("QSP", lambda e, gtt=gtt, x1l=x1l: e.dma_start(
                            out=x1l.bitcast(F32)[:, 0:1024],
                            in_=ctx["x1_sp"][gtt][:, 0:1024]),
                            r=kx1 + [("x1sp", gtt)], w=kx1, tag=f"p7c_x1a{gtt}")
                        em.op("QSP", lambda e, gtt=gtt, f1l=f1l: e.dma_start(
                            out=f1l, in_=ctx["ffn1_sp"][gtt]),
                            r=kf1 + [("f1sp", gtt)], w=kf1, tag=f"p7c_f1{gtt}")
                    if npo == 2:
                        em.op("QSP", lambda e, gtt=gtt, x1l=x1l: e.dma_start(
                            out=x1l.bitcast(F32)[:, 0:1024],
                            in_=ctx["x1_sp"][gtt][:, 1024:2048]),
                            r=kx1 + [("x1sp", gtt)], w=kx1, tag=f"p7c_x1b{gtt}")
                    yt, ytk = tmpf(512 * (npo % 2), 512 * (npo % 2 + 1))
                    em.dve(lambda e, yt=yt, npo=npo, x1l=x1l: e.tensor_tensor(
                        yt, ps(6 + npo % 2),
                        x1l.bitcast(F32)[:, 512 * (npo % 2):512 * (npo % 2 + 1)],
                        ALU.add),
                        r=[("PS", 6 + npo % 2)] + kx1, w=ytk, tag=f"p7c_y1{gtt}_{npo}")
                    em.dve(lambda e, yt=yt, npo=npo, f1l=f1l: e.tensor_tensor(
                        yt, yt, f1l[:, PN * npo:PN * (npo + 1)], ALU.add),
                        r=ytk + kf1, w=ytk, tag=f"p7c_y2{gtt}_{npo}")
                    em.op("QSP", lambda e, yt=yt, gtt=gtt, npo=npo: e.dma_start(
                        out=ctx["y"][128 * gtt:128 * (gtt + 1),
                                     PN * npo:PN * (npo + 1)], in_=yt),
                        r=ytk, w=[("yout", gtt, npo)], tag=f"p7c_ysp{gtt}_{npo}")

    if stage >= 3:
        p5_emit()
        if "p3" in dbg:
            O("d_go", (KT, 128, NT), BF16)
            for h in range(KT):
                em.op("QACT", lambda e, h=h: e.dma_start(
                    out=ctx["d_go"][h], in_=kvm_t(h)),
                    r=[("AR2", h)], w=[("dbg", "go", h)], tag="dbg_go")
        p4_emit()
        if "p3" in dbg:
            O("d_h", (128, HT * PN), BF16)
            em.op("QACT", lambda e: e.dma_start(
                out=ctx["d_h"], in_=sb["AR2"][:, 0:HT * PN]),
                r=[("h0", kh) for kh in range(HT)], w=[("dbg", "h")], tag="dbg_h")
        p6_emit()
    if stage >= 4:
        p0b_emit()
        if "only_p0b" not in dbg:
            p7a_emit()
            if "upto_p7a" not in dbg:
                p7b_emit()
                if "upto_p7b" not in dbg:
                    p8_emit(nodown=("p8nd" in dbg), onechunk=("p81c" in dbg))
                    if "upto_p8" not in dbg:
                        p7c_emit()
        if "p8y" in dbg:
            O("d_y1", (4, 128, D), BF16)
            for tt4 in range(4):
                em.op("QACT", lambda e, tt4=tt4: e.dma_start(
                    out=ctx["d_y1"][tt4],
                    in_=sb["AR4"][:, 2048 * tt4:2048 * (tt4 + 1)]),
                    r=[("AR4", tt4)], w=[("dbg", "y1", tt4)], tag="dbg_y1")
        if "p8h" in dbg:
            O("d_h1", (128, HT * PN), BF16)
            em.op("QACT", lambda e: e.dma_start(
                out=ctx["d_h1"], in_=sb["AR2"][:, 0:HT * PN]),
                r=[("h0", kh) for kh in range(HT)], w=[("dbg", "h1")], tag="dbg_h1")
        if "dqchk" in dbg:
            O("d_wgu", (2, 128, 2 * HID), BF16)
            for t in (0, 9):
                em.op("QACT", lambda e, t=t: e.dma_start(
                    out=ctx["d_wgu"][0 if t == 0 else 1], in_=ctx["wgu_sp"][t]),
                    r=[("wsp", "wgu_sp", t, cc) for cc in range(6)],
                    w=[("dbg", "wgu", t)], tag="dbg_wgu")
            O("d_wdn", (44, 128, D), BF16)
            for t in range(44):
                em.op("QACT", lambda e, t=t: e.dma_start(
                    out=ctx["d_wdn"][t], in_=ctx["wdn_sp"][t]),
                    r=[("wsp", "wdn_sp", t, 0)],
                    w=[("dbg", "wdn", t)], tag="dbg_wdn")

    if "st" in dbg:
        O("d_st", (128, 8), F32)
        O("d_xtm", (128, 2 * D), BF16)
        em.op("QACT", lambda e: e.dma_start(out=ctx["d_st"], in_=sb["p1_st"][:]),
              r=[("p1", "sum", 0), ("p1", "rstd", 0), ("p1", "var", 0),
                 ("p1", "ssq", 0), ("p1", "sum", 1), ("p1", "rstd", 1)],
              w=[("dbg", "st")], tag="dbg_st")
        em.op("QACT", lambda e: e.dma_start(out=ctx["d_xtm"], in_=sb["p1_xn_tm"][:]),
              r=[("p1", "xn", 0), ("p1", "xn", 1)], w=[("dbg", "xtm")], tag="dbg_xtm")
    if "p2" in dbg:
        O("d_mem", (KT, 128, NT), BF16)
        for h in range(KT):
            em.op("QACT", lambda e, h=h: e.dma_start(
                out=ctx["d_mem"][h], in_=kvm_t(h)),
                r=[("AR2", h)], w=[("dbg", "mem", h)], tag="dbg_mem")
        O("d_gam", (16, NT), F32)
        em.op("QACT", lambda e: e.dma_start(out=ctx["d_gam"],
                                            in_=sb["grows"][0:16, 0:NT]),
              r=[("grows", 0), ("grows", 1), ("grows", 2), ("grows", 3)],
              w=[("dbg", "gam")], tag="dbg_gam")
        O("d_pb", (16, NT), BF16)
        em.op("QACT", lambda e: e.dma_start(out=ctx["d_pb"],
                                            in_=sb["growsb"][0:16, NT:2 * NT]),
              r=["pbf"], w=[("dbg", "pb")], tag="dbg_pb")
        O("d_q", (KT, 128, NT), BF16)
        O("d_og", (KT, 128, NT), BF16)
        O("d_xc", (KT, 128, NT), BF16)
        for h in range(KT):
            em.op("QACT", lambda e, h=h: e.dma_start(
                out=ctx["d_q"][h], in_=ctx["q0_sp"][h]),
                r=[("q0sp", h)], w=[("dbg", "q", h)], tag="dbg_q")
            em.op("QACT", lambda e, h=h: e.dma_start(
                out=ctx["d_og"][h], in_=ctx["og_sp"][h]),
                r=[("ogsp", h)], w=[("dbg", "og", h)], tag="dbg_og")
            em.op("QACT", lambda e, h=h: e.dma_start(
                out=ctx["d_xc"][h], in_=ctx["xc_sp"][h]),
                r=[("xcsp", h)], w=[("dbg", "xc", h)], tag="dbg_xc")
        O("d_carry", (128, 16), F32)
        em.op("QACT", lambda e: e.dma_start(out=ctx["d_carry"], in_=sb["carry0"][:]),
              r=["carry0"], w=[("dbg", "carry")], tag="dbg_carry")
        O("d_ccs", (128, 32), F32)
        em.op("QACT", lambda e: e.dma_start(out=ctx["d_ccs"], in_=sb["ccs"][:]),
              r=[("ccs", h) for h in range(16)] + [("ccs", "a")],
              w=[("dbg", "ccs")], tag="dbg_ccs")
    if "p3" in dbg:
        O("d_a0", (KT, 128, D), BF16)
        O("d_f0", (KT, 128, D), BF16)
        O("d_x1", (KT, 128, D), F32)
        O("d_xn1", (KT, 128, NT), BF16)
        for tt in range(KT):
            em.op("QACT", lambda e, tt=tt: e.dma_start(
                out=ctx["d_a0"][tt], in_=ctx["attn0_sp"][tt]),
                r=[("a0sp", tt, npo) for npo in range(4)], w=[("dbg", "a0", tt)],
                tag="dbg_a0")
            em.op("QACT", lambda e, tt=tt: e.dma_start(
                out=ctx["d_f0"][tt], in_=ctx["ffn0_sp"][tt]),
                r=[("f0sp", tt, npo) for npo in range(4)], w=[("dbg", "f0", tt)],
                tag="dbg_f0")
            em.op("QACT", lambda e, tt=tt: e.dma_start(
                out=ctx["d_x1"][tt], in_=ctx["x1_sp"][tt]),
                r=[("x1sp", tt)], w=[("dbg", "x1", tt)], tag="dbg_x1")
        for kb in range(KT):
            em.op("QACT", lambda e, kb=kb: e.dma_start(
                out=ctx["d_xn1"][kb], in_=xn_t(kb)),
                r=[("AR0", kb)], w=[("dbg", "xn1", kb)], tag="dbg_xn1")
    if "p4d" in dbg:
        O("d_int", (KT, 128, D), BF16)
        O("d_f1", (KT, 128, D), BF16)
        O("d_sl", (8, 64, D), BF16)
        O("d_q1", (4, 64, 16384), BF16)
        for tt in range(KT):
            em.op("QACT", lambda e, tt=tt: e.dma_start(
                out=ctx["d_int"][tt], in_=ctx["intra_sp"][tt]),
                r=[("isp", tt)], w=[("dbg", "int", tt)], tag="dbg_int")
            em.op("QACT", lambda e, tt=tt: e.dma_start(
                out=ctx["d_f1"][tt], in_=ctx["ffn1_sp"][tt]),
                r=[("f1sp", tt)], w=[("dbg", "f1", tt)], tag="dbg_f1")
        for bq in range(8):
            em.op("QACT", lambda e, bq=bq: e.dma_start(
                out=ctx["d_sl"][bq], in_=ctx["sl_sp"][bq]),
                r=[("slsp", bq, h) for h in range(H1)], w=[("dbg", "sl", bq)],
                tag="dbg_sl")
        for cq in range(4):
            em.op("QACT", lambda e, cq=cq: e.dma_start(
                out=ctx["d_q1"][cq], in_=ctx["q1_sp"][cq]),
                r=[("q1sp", cq)], w=[("dbg", "q1", cq)], tag="dbg_q1")
    if "xn" in dbg:
        O("d_xn", (KT, 128, NT), BF16)
        for kb in range(KT):
            em.op("QACT", lambda e, kb=kb: e.dma_start(
                out=ctx["d_xn"][kb], in_=xn_t(kb)),
                r=[("AR0", kb)], w=[("dbg", "xn", kb)], tag="dbg_xn")
    if "xctx" in dbg:
        O("d_xctx", (128, KT * 3), BF16)
        em.op("QACT", lambda e: e.dma_start(out=ctx["d_xctx"], in_=sb["xc_ctx"][:]),
              r=[("xctx", kb) for kb in range(KT)], w=[("dbg", "xctx")],
              tag="dbg_xctx")
    if stage < 4:
        em.dve(lambda e: e.memset(sb["p1_x"][:, 0:D], 0.0), r=[("p1", "x", 0), ("p1", "x", 1)],
               w=[("p1", "x", 0)], tag="y_z")
        for tt in range(KT):
            em.op("QSP", lambda e, tt=tt: e.dma_start(
                out=ctx["y"][128 * tt:128 * (tt + 1), :], in_=sb["p1_x"][:, 0:D]),
                r=[("p1", "x", 0)], w=[("yout", tt)], tag="y_dummy")

    allres = list(em.lastw)
    em.pool(lambda e: e.engine_nop(), r=allres, tag="terminal")

    with nc.Block() as block:
        em.emit(nc, block)
    return nc, em, ctx


def kernel(**inputs):
    cores = host_prep(inputs)
    nc, em, _ = build()
    res = run_bass_kernel_spmd(nc, cores, core_ids=list(range(NCORES)))
    out = np.empty((2, 8192, 2048), np.float32)
    for c in range(NCORES):
        b, qq = divmod(c, 4)
        out[b, NT * qq:NT * (qq + 1)] = res.results[c]["y"]
    return out
